# version 01 (final)
# baseline (speedup 1.0000x reference)
import os
import numpy as np
from contextlib import ExitStack

import concourse.bass as bass
import concourse.mybir as mybir
from concourse.bass_core_rust import IndirectOffsetOnAxis
from concourse.bass_utils import run_bass_kernel_spmd

F32 = mybir.dt.float32
BF16 = mybir.dt.bfloat16
I32 = mybir.dt.int32
AF = mybir.ActivationFunctionType
ALU = mybir.AluOpType

NCORE = 8
B, S, C, H, Dh = 4, 4096, 1024, 16, 64
TOK = 2048
NTILE = TOK // 128
E, CAP, DHID = 8, 2560, 4096
NTOT = B * S
NCAPT = CAP // 128

_L = np.float32(np.log(10000.0))
LN1E4_HI = float(_L)
LN1E4_LO = float(np.float32(np.log(10000.0) - np.float64(_L)))
_P = np.float32(np.pi / 2)
PIO2_HI = float(_P)
PIO2_LO = float(np.float32(np.pi / 2 - np.float64(_P)))
TWOPI = float(np.float32(2 * np.pi))

ENGS = ("sync", "tensor", "vector", "scalar", "gpsimd")
_E2T = {"sync": mybir.EngineType.SP, "tensor": mybir.EngineType.PE,
        "vector": mybir.EngineType.DVE, "scalar": mybir.EngineType.Activation,
        "gpsimd": mybir.EngineType.Pool}


class Sched:
    def __init__(self, nc):
        self.nc = nc
        self.ops = {e: [] for e in ENGS}
        self.semval = {}
        self.sems = {}
        self.maxwait = {}

    def emit(self, eng, fn, waits=(), inc=None, inc_by=1, post_drain=False):
        w = []
        for ev in waits:
            if ev is None:
                continue
            sname, val, peng = ev
            if peng == eng and not (sname.startswith("d") or sname == "CC"):
                continue
            key = (eng, sname)
            if self.maxwait.get(key, -1) >= val:
                continue
            self.maxwait[key] = val
            w.append((sname, val))
        self.ops[eng].append((fn, w, inc, inc_by, post_drain))
        if inc is not None:
            self.semval[inc] = self.semval.get(inc, 0) + inc_by
            return (inc, self.semval[inc], eng)
        return None

    def dma(self, eng, out, in_, waits=(), tag=None, slow=False):
        sem = {"sync": "dS", "gpsimd": "dG", "scalar": "dA"}[eng]

        def fn(e, out=out, in_=in_, slow=slow):
            if slow:
                return e.dma_start(out=out, in_=in_, allow_slow_non_contiguous=True)
            return e.dma_start(out=out, in_=in_)
        fn._tag = tag or f"dma {getattr(out.tensor, 'name', '?')}<-{getattr(in_.tensor, 'name', '?')}"
        return self.emit(eng, fn, waits, inc=sem, inc_by=16)

    def replay(self, block):
        for eng in ENGS:
            if not self.ops[eng]:
                continue

            def make(body_eng):
                def body(e):
                    for _oi, (fn, w, inc, by, pdr) in enumerate(self.ops[body_eng]):
                        for sname, val in w:
                            e.wait_ge(self.sems[sname], val)
                        try:
                            inst = fn(e)
                            if pdr:
                                inst = e.drain()
                        except Exception as ex:
                            import traceback
                            traceback.print_exc()
                            raise RuntimeError(
                                f"replay fail eng={body_eng} op#{_oi} tag={getattr(fn, '_tag', None)}: {ex}") from ex
                        if inc is not None and inst is not None:
                            inst.then_inc(self.sems[inc], by)
                return body
            getattr(block, eng)(make(eng))


def build_nc(dbg=(), phases=99, DUMMY_GATHER=False, P7SUB=9):
    nc = bass.Bass("TRN2", target_bir_lowering=False, detect_race_conditions=False)

    xb = nc.dram_tensor("xb", [S, C], F32, kind="ExternalInput").ap()
    xq = nc.dram_tensor("xq", [TOK, C], F32, kind="ExternalInput").ap()
    wq_i = nc.dram_tensor("wq", [C, C], F32, kind="ExternalInput").ap()
    wk_i = nc.dram_tensor("wk", [C, C], F32, kind="ExternalInput").ap()
    wv_i = nc.dram_tensor("wv", [C, C], F32, kind="ExternalInput").ap()
    wo_i = nc.dram_tensor("wo", [C, C], F32, kind="ExternalInput").ap()
    pope = nc.dram_tensor("pope", [H, Dh], F32, kind="ExternalInput").ap()
    rw_i = nc.dram_tensor("rw", [C, E], F32, kind="ExternalInput").ap()
    rb_i = nc.dram_tensor("rbv", [1, E], F32, kind="ExternalInput").ap()
    w1_i = nc.dram_tensor("w1e", [C, DHID], F32, kind="ExternalInput").ap()
    b1_i = nc.dram_tensor("b1e", [1, DHID], F32, kind="ExternalInput").ap()
    w2_i = nc.dram_tensor("w2e", [DHID, C], F32, kind="ExternalInput").ap()
    b2_i = nc.dram_tensor("b2e", [1, C], F32, kind="ExternalInput").ap()
    meta = nc.dram_tensor("meta", [128, 4], F32, kind="ExternalInput").ap()

    out_t = nc.dram_tensor("out", [TOK, C], F32, kind="ExternalOutput").ap()

    wTd = {nm: {hl: nc.dram_tensor(f"w{nm}T{hl}", [C, C], BF16, kind="Internal").ap()
                for hl in ("h", "l")} for nm in ("q", "k", "v", "o")}
    xfh_d = nc.dram_tensor("xfh_d", [C, S], BF16, kind="Internal").ap()
    xfl_d = nc.dram_tensor("xfl_d", [C, S], BF16, kind="Internal").ap()
    ctab_d = nc.dram_tensor("ctab_d", [128, S], F32, kind="Internal").ap()
    stab_d = nc.dram_tensor("stab_d", [128, S], F32, kind="Internal").ap()
    v_d = nc.dram_tensor("v_d", [S, C], F32, kind="Internal").ap()
    ao_d = nc.dram_tensor("ao_d", [TOK, C], F32, kind="Internal").ap()
    x1_buf = nc.dram_tensor("x1_buf", [TOK, C], F32, kind="Internal").ap()
    mT_buf = nc.dram_tensor("mT_buf", [E, TOK], F32, kind="Internal").ap()
    x1_all = nc.dram_tensor("x1_all", [NTOT, C], F32, kind="Internal").ap()
    mT_all = nc.dram_tensor("mT_all", [NCORE * E, TOK], F32, kind="Internal").ap()
    cum_d = nc.dram_tensor("cum_d", [E, NTOT], F32, kind="Internal").ap()
    out2_b = nc.dram_tensor("out2_b", [CAP, C], F32, kind="Internal").ap()
    out2_all = nc.dram_tensor("out2_all", [E * CAP, C], F32, kind="Internal").ap()

    dbg_shapes = {"x1": [TOK, C], "gates": [128, NTILE * E], "mask": [E, TOK],
                  "ao": [TOK, C], "cum": [E, NTOT], "idx": [128, 20],
                  "out2": [CAP, C], "vdbg": [S, C],
                  "xed": [512, C], "hd": [DHID, 512], "ridx": [128, 2 * NTILE],
                  "tabs": [256, S], "colab": [128, 80]}
    dbg_t = {n: nc.dram_tensor("dbg_" + n, dbg_shapes[n], F32, kind="ExternalOutput").ap()
             for n in dbg}

    ident_i = nc.inline_tensor(np.eye(128, dtype=np.float32), name="ident").ap()
    colw_i = nc.inline_tensor(
        np.broadcast_to(2.0 ** -np.arange(E, dtype=np.float32), (128, E)).copy(), name="colw").ap()
    ecap_i = nc.inline_tensor(
        np.broadcast_to(np.arange(E, dtype=np.float32) * CAP, (128, E)).copy(), name="ecap").ap()
    iota_i = nc.inline_tensor(np.arange(128, dtype=np.float32).reshape(128, 1), name="iota128").ap()
    ones_r = nc.inline_tensor(np.ones((1, 128), dtype=np.float32), name="onesrow").ap()
    ones_c = nc.inline_tensor(np.ones((128, 1), dtype=np.float32), name="onescol").ap()

    sch = Sched(nc)

    with ExitStack() as _stk:
        def SB(name, shape, dt=F32):
            return _stk.enter_context(nc.sbuf_tensor(name, shape, dt))
        ids = SB("ids", [128, 128])
        meta_sb = SB("meta_sb", [128, 4])
        colsm = SB("colsm", [128, 80])
        gates_all = SB("gates_all", [128, NTILE * E])
        oh1_all = SB("oh1_all", [128, NTILE * E])
        oh2_all = SB("oh2_all", [128, NTILE * E])
        cw8 = SB("cw8", [128, E])
        ec8 = SB("ec8", [128, E])
        rbb = SB("rbb", [128, E])
        rwsl = SB("rwsl", [128, 2 * 64], BF16)
        idx_sb = SB("idx_sb", [128, 20], I32)
        i128 = SB("i128", [128, 1])
        onesr = SB("onesr", [128, 128])
        onesc = SB("onesc", [128, 1])
        pio2c = SB("pio2c", [128, 1])
        epsc = SB("epsc", [128, 1])
        PS = [_stk.enter_context(nc.psum_tensor(f"ps{i}", [128, 512], F32))
              for i in range(8)]
        ps7 = PS[7]
        block = _stk.enter_context(nc.Block())
        for sname in ("dS", "dG", "dA", "T", "V", "A", "G", "CC"):
            sch.sems[sname] = nc.alloc_semaphore(sname)

        EMIT, DMA = sch.emit, sch.dma

        def t_mm(ps_ap, lhsT, rhs, start, stop, waits=(), inc=None):
            def fn(e, o=ps_ap, a=lhsT, b=rhs, st=start, sp=stop):
                return e.matmul(o, a, b, start=st, stop=sp)
            return EMIT("tensor", fn, waits, inc=inc)

        def t_tr(ps_ap, in_ap, waits=(), inc=None):
            def fn(e, o=ps_ap, i=in_ap):
                return e.transpose(o, i, ids[:, :])
            return EMIT("tensor", fn, waits, inc=inc)

        def _outsz(args, kw):
            ap = kw.get("out")
            if ap is None and args:
                ap = args[0]
            try:
                return ap.free_size()
            except Exception:
                return 999999

        def v_op(name, *args, waits=(), inc=None, **kw):
            def fn(e, nm=name, a=args, k=kw):
                return getattr(e, nm)(*a, **k)
            return EMIT("vector", fn, waits, inc=inc,
                        post_drain=(_outsz(args, kw) < 256) or name == "tensor_tensor_scan")

        def a_op(name, *args, waits=(), inc=None, **kw):
            def fn(e, nm=name, a=args, k=kw):
                return getattr(e, nm)(*a, **k)
            return EMIT("scalar", fn, waits, inc=inc,
                        post_drain=_outsz(args, kw) < 256)

        def g_opx(name, *args, waits=(), inc=None, inc_by=1, **kw):
            def fn(e, nm=name, a=args, k=kw):
                return getattr(e, nm)(*a, **k)
            return EMIT("gpsimd", fn, waits, inc=inc, inc_by=inc_by)

        def mm3(ps_ap, ah, al, bh, bl, first, last, waits=(), inc=None):
            t_mm(ps_ap, ah, bh, first, False, waits=waits)
            t_mm(ps_ap, ah, bl, False, False)
            return t_mm(ps_ap, al, bh, False, last, inc=inc)

        def split(src, dsth, dstl, scratch, waits=()):
            e1 = v_op("tensor_copy", out=dsth, in_=src, waits=waits, inc="V")
            v_op("tensor_tensor", out=scratch, in0=src, in1=dsth, op=ALU.subtract)
            e2 = v_op("tensor_copy", out=dstl, in_=scratch, inc="V")
            return e1, e2

        def phase_sync():
            evs = []
            for sname in ("dS", "dG", "dA", "T", "V", "A", "G", "CC"):
                if sch.semval.get(sname):
                    evs.append((sname, sch.semval[sname], "__none__"))
            for eng in ENGS:
                for ev in evs:
                    sname, val, _ = ev
                    key = (eng, sname)
                    if sch.maxwait.get(key, -1) >= val:
                        continue
                EMIT(eng, lambda e: e.nop(), waits=tuple(evs))

        pid_cache = {}

        def core_reg(eng):
            if eng not in pid_cache:
                pid_cache[eng] = nc.partition_id(engines=[_E2T[eng]])
            return pid_cache[eng]

        e_ids = DMA("sync", ids[:, :], ident_i[:, :])
        DMA("sync", meta_sb[:, :], meta[:, :])
        DMA("sync", cw8[:, :], colw_i[:, :])
        e_ec8 = DMA("sync", ec8[:, :], ecap_i[:, :])
        e_i128 = DMA("sync", i128[:, :], iota_i[:, :])
        e_onesc = DMA("sync", onesc[:, :], ones_c[:, :])
        e_onesr = DMA("sync", onesr[0:1, :], ones_r[:, :])
        v_op("memset", pio2c[:, :], PIO2_HI)
        e_pio2 = v_op("memset", epsc[:, :], 1e-5, inc="V")

        with ExitStack() as _s0:
            def SB0(name, shape, dt=F32):
                return _s0.enter_context(nc.sbuf_tensor(name, shape, dt))
            stA = SB0("stA", [128, 1024])
            stB = SB0("stB", [128, 1024])
            w_hi = SB0("w_hi", [128, 1024], BF16)
            w_lo = SB0("w_lo", [128, 1024], BF16)
            tC = SB0("tC", [128, S])
            tS = SB0("tS", [128, S])
            tcol = SB0("tcol", [128, 16])
            ttmp = SB0("ttmp", [128, 64])
            pbT = SB0("pbT", [128, 16])
            e_rbl = DMA("sync", stA[0:1, 0:E], rb_i[:, :])
            ev = t_mm(ps7[:, 0:E], onesr[0:1, :], stA[0:1, 0:E], True, True,
                      waits=(e_rbl, e_onesr), inc="T")
            e_rbb = v_op("tensor_copy", out=rbb[:, :], in_=ps7[:, 0:E], waits=(ev,), inc="V")
            e_rwl = DMA("sync", stB[:, 0:64].rearrange("p (cc e) -> p cc e", cc=8),
                        rw_i.rearrange("(cc p) e -> p cc e", p=128), waits=(e_rbb,))
            _, e_rw = split(stB[:, 0:64], rwsl[:, 0:64], rwsl[:, 64:128],
                            stA[:, 0:64], waits=(e_rwl,))

            e_ws = e_rw
            tr_bank = {}
            for nm, src in (("q", wq_i), ("k", wk_i), ("v", wv_i), ("o", wo_i)):
                for ot in range(8):
                    e_l = DMA("sync", stA[:, :], src[ot * 128:(ot + 1) * 128, :],
                              waits=(e_ws,))
                    ev_t = None
                    for cc in range(8):
                        pst = PS[cc % 2][:, 0:128]
                        ev_t = t_tr(pst, stA[:, cc * 128:(cc + 1) * 128],
                                    waits=(e_l, e_ids, tr_bank.get(cc % 2)), inc="T")
                        tr_bank[cc % 2] = v_op("tensor_copy",
                                               out=stB[:, cc * 128:(cc + 1) * 128],
                                               in_=pst, waits=(ev_t,), inc="V")
                    e_h, e_lo = split(stB[:, :], w_hi[:, :], w_lo[:, :], stA[:, :],
                                      waits=(ev_t,))
                    DMA("sync",
                        wTd[nm]["h"].rearrange("(cc p) o -> p cc o", p=128)[:, :, ot * 128:(ot + 1) * 128],
                        w_hi.rearrange("p (cc o) -> p cc o", cc=8), waits=(e_h,))
                    e_ws = DMA("sync",
                               wTd[nm]["l"].rearrange("(cc p) o -> p cc o", p=128)[:, :, ot * 128:(ot + 1) * 128],
                               w_lo.rearrange("p (cc o) -> p cc o", cc=8), waits=(e_lo,))
            e_wT_done = e_ws

            v_op("tensor_scalar_mul", tcol[0:64, 0:1], i128[0:64, 0:1], 1.0 / 64.0,
                 waits=(e_i128,))
            v_op("tensor_scalar_mul", tcol[0:64, 1:2], tcol[0:64, 0:1], LN1E4_HI)
            v_op("tensor_scalar_mul", tcol[0:64, 2:3], tcol[0:64, 0:1], LN1E4_LO)
            ev = v_op("tensor_add", tcol[0:64, 1:2], tcol[0:64, 1:2], tcol[0:64, 2:3], inc="V")
            ev = a_op("activation", out=tcol[0:64, 0:1], in_=tcol[0:64, 1:2], func=AF.Exp,
                      scale=-1.0, waits=(ev,), inc="A")
            ev_s = a_op("activation", out=tS[0:64, 1:2], in_=tcol[0:64, 0:1], func=AF.Sin,
                        waits=(ev,), inc="A")
            ev2 = v_op("tensor_scalar_add", tcol[0:64, 3:4], tcol[0:64, 0:1], PIO2_LO,
                       waits=(ev,), inc="V")
            ev_c = a_op("activation", out=tC[0:64, 1:2], in_=tcol[0:64, 3:4], func=AF.Sin,
                        bias=pio2c[0:64, 0:1], waits=(ev2, e_pio2), inc="A")
            v_op("memset", tC[0:64, 0:1], 1.0)
            v_op("memset", tS[0:64, 0:1], 0.0)
            v_op("tensor_copy", out=tcol[0:64, 1:2], in_=tC[0:64, 1:2], waits=(ev_c,))
            v_op("tensor_copy", out=tcol[0:64, 2:3], in_=tS[0:64, 1:2], waits=(ev_s,))
            v_op("tensor_copy", out=tcol[0:64, 3:4], in_=tcol[0:64, 1:2])
            v_op("tensor_copy", out=tcol[0:64, 4:5], in_=tcol[0:64, 2:3])
            for rep in range(6):
                v_op("tensor_mul", tcol[0:64, 7:8], tcol[0:64, 3:4], tcol[0:64, 3:4])
                v_op("tensor_mul", tcol[0:64, 8:9], tcol[0:64, 4:5], tcol[0:64, 4:5])
                v_op("tensor_mul", tcol[0:64, 9:10], tcol[0:64, 3:4], tcol[0:64, 4:5])
                v_op("tensor_tensor", out=tcol[0:64, 3:4], in0=tcol[0:64, 7:8],
                     in1=tcol[0:64, 8:9], op=ALU.subtract)
                v_op("tensor_scalar_mul", tcol[0:64, 4:5], tcol[0:64, 9:10], 2.0)
                if rep == 2:
                    v_op("tensor_copy", out=tcol[0:64, 5:6], in_=tcol[0:64, 3:4])
                    v_op("tensor_copy", out=tcol[0:64, 6:7], in_=tcol[0:64, 4:5])
            def rot_step(dst_sl, src_sl, ccol, scol, tmp_sl):
                v_op("tensor_scalar", out=ttmp[0:64, tmp_sl], in0=tS[0:64, src_sl],
                     scalar1=scol, scalar2=None, op0=ALU.mult)
                v_op("scalar_tensor_tensor", out=tC[0:64, dst_sl], in0=tC[0:64, src_sl],
                     scalar=ccol, in1=ttmp[0:64, tmp_sl], op0=ALU.mult, op1=ALU.subtract)
                v_op("tensor_scalar", out=ttmp[0:64, tmp_sl], in0=tC[0:64, src_sl],
                     scalar1=scol, scalar2=None, op0=ALU.mult)
                v_op("scalar_tensor_tensor", out=tS[0:64, dst_sl], in0=tS[0:64, src_sl],
                     scalar=ccol, in1=ttmp[0:64, tmp_sl], op0=ALU.mult, op1=ALU.add)
            for s in range(2, 16):
                rot_step(slice(s, s + 1), slice(s - 1, s),
                         tcol[0:64, 1:2], tcol[0:64, 2:3], slice(0, 1))
            for blk in range(2, 16):
                s0 = blk * 8
                rot_step(slice(s0, s0 + 8), slice(s0 - 8, s0),
                         tcol[0:64, 5:6], tcol[0:64, 6:7], slice(0, 8))
            for blk in range(1, 64):
                s0 = blk * 64
                rot_step(slice(s0, s0 + 64), slice(s0 - 64, s0),
                         tcol[0:64, 3:4], tcol[0:64, 4:5], slice(0, 64))
            ev_tab = v_op("memset", tcol[0:64, 15:16], 0.0, inc="V")
            DMA("sync", ctab_d[0:64, :], tC[0:64, :], waits=(ev_tab,))
            DMA("sync", ctab_d[64:128, :], tC[0:64, :], waits=(ev_tab,))
            DMA("sync", stab_d[0:64, :], tS[0:64, :], waits=(ev_tab,))
            e_tabd = DMA("sync", stab_d[64:128, :], tS[0:64, :], waits=(ev_tab,))

            if "tabs" in dbg_t:
                DMA("gpsimd", dbg_t["tabs"][0:128, :], ctab_d[:, :], waits=(e_tabd,))
                DMA("gpsimd", dbg_t["tabs"][128:256, :], stab_d[:, :], waits=(e_tabd,))
            ev0 = v_op("memset", stA[:, 0:128], 0.0, inc="V")
            e_pl = DMA("sync", stB[0:16, 0:Dh], pope[:, :], waits=(ev0,))
            ev = v_op("tensor_scalar_max", stA[0:16, 0:Dh], stB[0:16, 0:Dh], -TWOPI,
                      waits=(e_pl,))
            ev = v_op("tensor_scalar_min", stA[0:16, 0:Dh], stA[0:16, 0:Dh], 0.0)
            ev = v_op("tensor_scalar_mul", stA[0:16, 0:Dh], stA[0:16, 0:Dh], 0.5, inc="V")
            ev_sh = a_op("activation", out=stB[0:16, 0:Dh], in_=stA[0:16, 0:Dh],
                         func=AF.Sin, waits=(ev,), inc="A")
            ev2 = v_op("tensor_scalar_add", stA[0:16, 64:128], stA[0:16, 0:Dh], PIO2_LO,
                       waits=(ev,), inc="V")
            ev_ch = a_op("activation", out=stB[0:16, 64:128], in_=stA[0:16, 64:128],
                         func=AF.Sin, bias=pio2c[0:16, 0:1], waits=(ev2, e_pio2), inc="A")
            v_op("tensor_mul", stA[0:16, 0:Dh], stB[0:16, 0:Dh], stB[0:16, 0:Dh],
                 waits=(ev_sh,))
            v_op("tensor_scalar", out=stA[0:16, 0:Dh], in0=stA[0:16, 0:Dh], scalar1=-2.0,
                 scalar2=1.0, op0=ALU.mult, op1=ALU.add)
            v_op("tensor_mul", stA[0:16, 64:128], stB[0:16, 0:Dh], stB[0:16, 64:128],
                 waits=(ev_ch,))
            ev = v_op("tensor_scalar_mul", stA[0:16, 64:128], stA[0:16, 64:128], 2.0,
                      inc="V")
            evt = t_tr(ps7[:, 0:128], stA[:, 0:128], waits=(ev, e_ids), inc="T")
            ev = v_op("tensor_copy", out=pbT[:, 0:16], in_=ps7[:, 0:16], waits=(evt,),
                      inc="V")
            v_op("tensor_copy", out=colsm[0:64, 0:16], in_=pbT[0:64, 0:16])
            ev_a = v_op("tensor_scalar_mul", colsm[64:128, 0:16], pbT[64:128, 0:16],
                        -1.0, inc="V")
            DMA("gpsimd", colsm[0:64, 16:32], pbT[64:128, 0:16], waits=(ev,))
            e_AB = DMA("gpsimd", colsm[64:128, 16:32], pbT[0:64, 0:16], waits=(ev,))

            if "colab" in dbg_t:
                DMA("gpsimd", dbg_t["colab"][:, :], colsm[:, :], waits=(e_AB, ev_a))
            e_xf = e_AB
            for tt in range(32):
                e_l = DMA("sync", stA[:, :], xb[tt * 128:(tt + 1) * 128, :], waits=(e_xf,))
                ev_t = None
                for cc in range(8):
                    pst = PS[cc % 2][:, 0:128]
                    ev_t = t_tr(pst, stA[:, cc * 128:(cc + 1) * 128],
                                waits=(e_l, tr_bank.get(cc % 2)), inc="T")
                    tr_bank[cc % 2] = v_op("tensor_copy",
                                           out=stB[:, cc * 128:(cc + 1) * 128],
                                           in_=pst, waits=(ev_t,), inc="V")
                e_h, e_lo = split(stB[:, :], w_hi[:, :], w_lo[:, :], stA[:, :],
                                  waits=(ev_t,))
                DMA("sync", xfh_d.rearrange("(cc p) t -> p cc t", p=128)[:, :, tt * 128:(tt + 1) * 128],
                    w_hi.rearrange("p (cc t) -> p cc t", cc=8), waits=(e_h,))
                e_xf = DMA("sync", xfl_d.rearrange("(cc p) t -> p cc t", p=128)[:, :, tt * 128:(tt + 1) * 128],
                           w_lo.rearrange("p (cc t) -> p cc t", cc=8), waits=(e_lo,))
            e_xf_done = e_xf

        if phases >= 2:
            phase_sync()
            with ExitStack() as _s2:
                def SB2(name, shape, dt=F32):
                    return _s2.enter_context(nc.sbuf_tensor(name, shape, dt))
                xs_h = SB2("xs_h", [128, 8 * 512], BF16)
                xs_l = SB2("xs_l", [128, 8 * 512], BF16)
                wv_h = SB2("wv_h", [128, 8 * 1024], BF16)
                wv_l = SB2("wv_l", [128, 8 * 1024], BF16)
                vst = SB2("vst", [128, 1024])
                DMA("sync", wv_h.rearrange("p (cc o) -> p cc o", cc=8),
                    wTd["v"]["h"].rearrange("(cc p) o -> p cc o", p=128), waits=(e_wT_done,))
                e_wv = DMA("sync", wv_l.rearrange("p (cc o) -> p cc o", cc=8),
                           wTd["v"]["l"].rearrange("(cc p) o -> p cc o", p=128),
                           waits=(e_wT_done,))
                e_vs = None
                bankev = {}
                for tc in range(8):
                    e_x1 = DMA("sync", xs_h.rearrange("p (cc t) -> p cc t", cc=8),
                               xfh_d.rearrange("(cc p) t -> p cc t", p=128)[:, :, tc * 512:(tc + 1) * 512],
                               waits=(e_xf_done,))
                    e_x2 = DMA("sync", xs_l.rearrange("p (cc t) -> p cc t", cc=8),
                               xfl_d.rearrange("(cc p) t -> p cc t", p=128)[:, :, tc * 512:(tc + 1) * 512],
                               waits=(e_xf_done,))
                    for tt in range(4):
                        ev_e = None
                        for oh in range(2):
                            bk = 2 * (tt % 2) + oh
                            pst = PS[bk]
                            ev = None
                            for cc in range(8):
                                ah = xs_h[:, cc * 512 + tt * 128: cc * 512 + (tt + 1) * 128]
                                al = xs_l[:, cc * 512 + tt * 128: cc * 512 + (tt + 1) * 128]
                                bh = wv_h[:, cc * 1024 + oh * 512: cc * 1024 + (oh + 1) * 512]
                                bl = wv_l[:, cc * 1024 + oh * 512: cc * 1024 + (oh + 1) * 512]
                                ev = mm3(pst[:, :], ah, al, bh, bl, cc == 0, cc == 7,
                                         waits=(e_x1, e_x2, e_wv, bankev.get(bk)),
                                         inc="T" if cc == 7 else None)
                            ev_e = v_op("tensor_copy", out=vst[:, oh * 512:(oh + 1) * 512],
                                        in_=pst[:, :], waits=(ev, e_vs), inc="V")
                            bankev[bk] = ev_e
                        e_vs = DMA("sync", v_d[tc * 512 + tt * 128: tc * 512 + (tt + 1) * 128, :],
                                   vst[:, :], waits=(ev_e,))
                e_v_done = e_vs
                if "vdbg" in dbg_t:
                    e_v_done = DMA("gpsimd", dbg_t["vdbg"][:, :], v_d[:, :], waits=(e_v_done,))

        if phases >= 3:
            phase_sync()
            with ExitStack() as _s3:
                def SB3(name, shape, dt=F32):
                    return _s3.enter_context(nc.sbuf_tensor(name, shape, dt))
                tqCS = SB3("tqCS", [128, 2 * 512])
                tkCS = SB3("tkCS", [128, 2 * 2048])
                xs_h2 = SB3("xs_h2", [128, 8 * 512], BF16)
                xs_l2 = SB3("xs_l2", [128, 8 * 512], BF16)
                wsl = SB3("wsl", [128, 4 * 1024], BF16)
                magq = SB3("magq", [128, 2048])
                mgqx = SB3("mgqx", [128, 2048])
                qstk = SB3("qstk", [128, 2048])
                qhi = SB3("qhi", [128, 2 * 2048], BF16)
                qlo = SB3("qlo", [128, 2 * 2048], BF16)
                magk = SB3("magk", [128, 2048])
                mgkx = SB3("mgkx", [128, 2048])
                khi = SB3("khi", [128, 2 * 4096], BF16)
                klo = SB3("klo", [128, 2 * 4096], BF16)
                vhd = SB3("vhd", [128, 32 * 66])
                at0 = SB3("at0", [128, 512])
                at1 = SB3("at1", [128, 512])
                at2 = SB3("at2", [128, 512])
                at3 = SB3("at3", [128, 512])
                aoev = SB3("aoev", [128, 256])
                spst = SB3("spst", [128, 512])
                eqA = SB3("eqA", [128, 512])
                eqB = SB3("eqB", [128, 512])
                ekst = SB3("ekst", [128, 2048])
                spl = SB3("spl", [128, 2048])
                sqst = SB3("sqst", [128, 2048])
                ATT = [at0, at1, at2, at3]
                qb_g = core_reg("gpsimd")
                qbase_g = (qb_g % 2) * TOK

                ev_avE = {}
                ev_expA = {}
                ev_atfree = {}
                ev_sq_free = None
                e_tq_free = None
                e_tk_free = None
                e_xs2_free = None
                e_ao = None
                ev_ps6 = None
                ev_ps7 = None
                e_wsl_free = None

                for p in range(8):
                    h0 = 2 * p
                    for j, (nm, hl) in enumerate((("q", "h"), ("q", "l"), ("k", "h"), ("k", "l"))):
                        e_wsl = DMA("sync",
                                    wsl.rearrange("p (j cc o) -> p j cc o", j=4, cc=8)[:, j, :, :],
                                    wTd[nm][hl].rearrange("(cc p) o -> p cc o", p=128)[:, :, p * 128:(p + 1) * 128],
                                    waits=(e_wT_done, e_wsl_free))
                    Wq_h = wsl[:, 0:1024]
                    Wq_l = wsl[:, 1024:2048]
                    Wk_h = wsl[:, 2048:3072]
                    Wk_l = wsl[:, 3072:4096]

                    ev_mq = None
                    for qc in range(4):
                        e_x1 = DMA("gpsimd", xs_h2.rearrange("p (cc t) -> p cc t", cc=8),
                                   xfh_d.rearrange("(cc p) t -> p cc t", p=128)
                                   [:, :, bass.ds(qbase_g + qc * 512, 512)],
                                   waits=(e_xf_done, e_xs2_free))
                        e_x2 = DMA("gpsimd", xs_l2.rearrange("p (cc t) -> p cc t", cc=8),
                                   xfl_d.rearrange("(cc p) t -> p cc t", p=128)
                                   [:, :, bass.ds(qbase_g + qc * 512, 512)],
                                   waits=(e_xf_done,))
                        ev = None
                        for cc in range(8):
                            ev = mm3(PS[6][:, :],
                                     Wq_h[:, cc * 128:(cc + 1) * 128],
                                     Wq_l[:, cc * 128:(cc + 1) * 128],
                                     xs_h2[:, cc * 512:(cc + 1) * 512],
                                     xs_l2[:, cc * 512:(cc + 1) * 512],
                                     cc == 0, cc == 7,
                                     waits=(e_x1, e_x2, e_wsl, ev_ps6),
                                     inc="T" if cc == 7 else None)
                        e_xs2_free = ev
                        ev_ps6 = a_op("activation", out=spst[:, :], in_=PS[6][:, :],
                                      func=AF.Exp, waits=(ev,), inc="A")
                        ev_mq = a_op("activation", out=magq[:, qc * 512:(qc + 1) * 512],
                                     in_=spst[:, :], func=AF.Ln, bias=onesc[:, 0:1], inc="A")
                    DMA("gpsimd", mgqx[64:128, :], magq[0:64, :], waits=(ev_mq,))
                    e_gq = DMA("gpsimd", mgqx[0:64, :], magq[64:128, :], waits=(ev_mq,))

                    e_qsplit = {}
                    ev_qb2 = {}
                    for hh in range(2):
                        hname = h0 + hh
                        Acol = colsm[:, hname:hname + 1]
                        Bcol = colsm[:, 16 + hname:16 + hname + 1]
                        mt = magq if hh == 0 else mgqx
                        mb = mgqx if hh == 0 else magq
                        for qc in range(4):
                            e_t1 = DMA("gpsimd", tqCS[:, 0:512],
                                       ctab_d[:, bass.ds(qbase_g + qc * 512, 512)],
                                       waits=(e_tabd, e_tq_free))
                            e_t2 = DMA("gpsimd", tqCS[:, 512:1024],
                                       stab_d[:, bass.ds(qbase_g + qc * 512, 512)],
                                       waits=(e_tabd,))
                            sl = slice(qc * 512, (qc + 1) * 512)
                            v_op("tensor_tensor", out=eqA[0:64, :], in0=mt[0:64, sl],
                                 in1=tqCS[0:64, 0:512], op=ALU.mult,
                                 waits=(e_t1, e_gq, e_AB))
                            v_op("tensor_tensor", out=eqA[64:128, :], in0=mb[64:128, sl],
                                 in1=tqCS[64:128, 0:512], op=ALU.mult)
                            v_op("tensor_tensor", out=eqB[0:64, :], in0=mt[0:64, sl],
                                 in1=tqCS[0:64, 512:1024], op=ALU.mult, waits=(e_t2,))
                            e_tq_free = v_op("tensor_tensor", out=eqB[64:128, :],
                                             in0=mb[64:128, sl],
                                             in1=tqCS[64:128, 512:1024], op=ALU.mult,
                                             inc="V")
                            v_op("tensor_scalar", out=eqB[:, :], in0=eqB[:, :],
                                 scalar1=Bcol, scalar2=None, op0=ALU.mult)
                            ev_qbld = v_op("scalar_tensor_tensor", out=qstk[:, sl],
                                           in0=eqA[:, :], scalar=Acol, in1=eqB[:, :],
                                           op0=ALU.mult, op1=ALU.add, inc="V")
                        qsl = slice(hh * 2048, (hh + 1) * 2048)
                        _, e_ql = split(qstk[:, :], qhi[:, qsl], qlo[:, qsl], spl[:, :])
                        e_qsplit[hh] = e_ql
                        ev_sq = a_op("activation", out=sqst[:, :], in_=qstk[:, :],
                                     func=AF.Square, waits=(ev_sq_free, ev_qbld), inc="A")
                        ev_r = v_op("memset", colsm[0:1, 48:49], 0.0, waits=(ev_ps7,), inc="V")
                        for ch in range(4):
                            evm = t_mm(ps7[0:1, 0:512], onesc[:, :],
                                       sqst[:, ch * 512:(ch + 1) * 512], True, True,
                                       waits=(ev_sq, e_onesc, ev_r), inc="T")
                            v_op("tensor_reduce", out=colsm[0:1, 64:65], in_=ps7[0:1, 0:512],
                                 axis=mybir.AxisListType.X, op=ALU.max, waits=(evm,))
                            ev_r = v_op("tensor_tensor", out=colsm[0:1, 48:49],
                                        in0=colsm[0:1, 48:49], in1=colsm[0:1, 64:65],
                                        op=ALU.max, inc="V")
                            ev_sq_free = evm
                            ev_ps7 = ev_r
                        ev_qb2[hh] = ev_r
                        v_op("tensor_copy", out=colsm[0:1, 50 + hh:51 + hh],
                             in_=colsm[0:1, 48:49])

                    e_ksplit = {}
                    ev_mk2 = {0: None, 1: None}
                    ev_r = v_op("memset", colsm[0:1, 66:68], 0.0, waits=(ev_ps7,), inc="V")
                    ev_ps7 = ev_r
                    for half in range(2):
                        tb = half * 2048
                        ev_mk = None
                        for tc4 in range(4):
                            tg = tb + tc4 * 512
                            e_x1 = DMA("sync", xs_h2.rearrange("p (cc t) -> p cc t", cc=8),
                                       xfh_d.rearrange("(cc p) t -> p cc t", p=128)[:, :, tg:tg + 512],
                                       waits=(e_xf_done, e_xs2_free))
                            e_x2 = DMA("sync", xs_l2.rearrange("p (cc t) -> p cc t", cc=8),
                                       xfl_d.rearrange("(cc p) t -> p cc t", p=128)[:, :, tg:tg + 512],
                                       waits=(e_xf_done,))
                            ev = None
                            for cc in range(8):
                                ev = mm3(PS[6][:, :],
                                         Wk_h[:, cc * 128:(cc + 1) * 128],
                                         Wk_l[:, cc * 128:(cc + 1) * 128],
                                         xs_h2[:, cc * 512:(cc + 1) * 512],
                                         xs_l2[:, cc * 512:(cc + 1) * 512],
                                         cc == 0, cc == 7,
                                         waits=(e_x1, e_x2, e_wsl, ev_ps6),
                                         inc="T" if cc == 7 else None)
                            e_xs2_free = ev
                            ev_ps6 = a_op("activation", out=spst[:, :], in_=PS[6][:, :],
                                          func=AF.Exp, waits=(ev,), inc="A")
                            ev_mk = a_op("activation", out=magk[:, tc4 * 512:(tc4 + 1) * 512],
                                         in_=spst[:, :], func=AF.Ln, bias=onesc[:, 0:1], inc="A")
                        DMA("gpsimd", mgkx[64:128, :], magk[0:64, :], waits=(ev_mk,))
                        e_gk = DMA("gpsimd", mgkx[0:64, :], magk[64:128, :], waits=(ev_mk,))
                        e_tk1 = DMA("sync", tkCS[:, 0:2048], ctab_d[:, tb:tb + 2048],
                                    waits=(e_tabd, e_tk_free))
                        e_tk2 = DMA("sync", tkCS[:, 2048:4096], stab_d[:, tb:tb + 2048],
                                    waits=(e_tabd,))
                        for hh in range(2):
                            mt = magk if hh == 0 else mgkx
                            mb = mgkx if hh == 0 else magk
                            v_op("tensor_tensor", out=ekst[0:64, :], in0=mt[0:64, :],
                                 in1=tkCS[0:64, 0:2048], op=ALU.mult, waits=(e_tk1, e_gk))
                            ev_k = v_op("tensor_tensor", out=ekst[64:128, :],
                                        in0=mb[64:128, :],
                                        in1=tkCS[64:128, 2048:4096], op=ALU.mult,
                                        waits=(e_tk2,), inc="V")
                            e_tk_free = ev_k
                            ksl = slice(hh * 4096 + tb, hh * 4096 + tb + 2048)
                            _, e_kl = split(ekst[:, :], khi[:, ksl], klo[:, ksl], spl[:, :])
                            e_ksplit[hh] = e_kl
                            ev_ksq = a_op("activation", out=sqst[:, :], in_=ekst[:, :],
                                          func=AF.Square, waits=(ev_sq_free, ev_k), inc="A")
                            for ch in range(4):
                                evm = t_mm(ps7[0:1, 0:512], onesc[:, :],
                                           sqst[:, ch * 512:(ch + 1) * 512], True, True,
                                           waits=(ev_ksq, ev_ps7), inc="T")
                                v_op("tensor_reduce", out=colsm[0:1, 64:65],
                                     in_=ps7[0:1, 0:512], axis=mybir.AxisListType.X,
                                     op=ALU.max, waits=(evm,))
                                ev_r = v_op("tensor_tensor", out=colsm[0:1, 66 + hh:67 + hh],
                                            in0=colsm[0:1, 66 + hh:67 + hh],
                                            in1=colsm[0:1, 64:65], op=ALU.max, inc="V")
                                ev_sq_free = evm
                                ev_ps7 = ev_r
                                ev_mk2[hh] = ev_r

                    ev_negc = {}
                    for hh in range(2):
                        hname = h0 + hh
                        ev = v_op("tensor_tensor", out=colsm[0:1, 64:65],
                                  in0=colsm[0:1, 50 + hh:51 + hh],
                                  in1=colsm[0:1, 66 + hh:67 + hh], op=ALU.mult,
                                  waits=(ev_mk2[hh], ev_qb2[hh]), inc="V")
                        ev = a_op("activation", out=colsm[0:1, 64:65], in_=colsm[0:1, 64:65],
                                  func=AF.Ln, waits=(ev,), inc="A")
                        ev = a_op("activation", out=colsm[0:1, 64:65], in_=colsm[0:1, 64:65],
                                  func=AF.Exp, scale=0.5, inc="A")
                        ev = v_op("tensor_scalar_mul", colsm[0:1, 65:66], colsm[0:1, 64:65],
                                  -0.125, waits=(ev,), inc="V")
                        evb = t_mm(ps7[:, 0:1], onesr[0:1, :], colsm[0:1, 65:66], True, True,
                                   waits=(ev, ev_ps7), inc="T")
                        ev_negc[hh] = v_op("tensor_copy",
                                           out=colsm[:, 32 + hname:33 + hname],
                                           in_=ps7[:, 0:1], waits=(evb,), inc="V")
                        ev_ps7 = ev_negc[hh]

                    for hh in range(2):
                        hname = h0 + hh
                        e_vl = DMA("sync",
                                   vhd.rearrange("p (tt o) -> p tt o", tt=32)[:, :, 0:64],
                                   v_d.rearrange("(tt p) o -> p tt o", p=128)
                                   [:, :, hname * 64:(hname + 1) * 64],
                                   waits=(e_v_done, ev_avE.get(3)))
                        ev_ones = v_op("memset",
                                       vhd.rearrange("p (tt o) -> p tt o", tt=32)[:, :, 64:65],
                                       1.0, inc="V")
                        negC = colsm[:, 32 + hname:33 + hname]
                        for qc in range(4):
                            ev_av_stop = {}
                            for tt in range(32):
                                bk = 4 + (tt % 2)
                                psS = PS[bk]
                                ksl0 = hh * 4096 + tt * 128
                                qsl = slice(hh * 2048 + qc * 512, hh * 2048 + (qc + 1) * 512)
                                ev_s3 = mm3(psS[:, :],
                                            khi[:, ksl0:ksl0 + 128], klo[:, ksl0:ksl0 + 128],
                                            qhi[:, qsl], qlo[:, qsl],
                                            True, True,
                                            waits=(e_ksplit[hh], e_qsplit[hh],
                                                   ev_expA.get(bk)),
                                            inc="T")
                                at = ATT[tt % 4]
                                ev_exp = a_op("activation", out=at[:, :], in_=psS[:, :],
                                              func=AF.Exp, bias=negC, scale=0.125,
                                              waits=(ev_s3, ev_negc[hh],
                                                     ev_atfree.get(tt % 4)),
                                              inc="A")
                                ev_expA[bk] = ev_exp
                                for qt in range(4):
                                    want_inc = (qt == 3) or (tt == 31)
                                    evv = t_mm(PS[qt][:, 0:65],
                                               at[:, qt * 128:(qt + 1) * 128],
                                               vhd[:, tt * 66:tt * 66 + 65],
                                               tt == 0, tt == 31,
                                               waits=(ev_exp, e_vl, ev_ones,
                                                      ev_avE.get(qt) if tt == 0 else None),
                                               inc="T" if want_inc else None)
                                    if tt == 31:
                                        ev_av_stop[qt] = evv
                                ev_atfree[tt % 4] = evv
                            e_ao_prev = e_ao
                            for qt in range(4):
                                v_op("reciprocal", out=colsm[:, 64:65],
                                     in_=PS[qt][:, 64:65],
                                     waits=(ev_av_stop[qt], e_ao_prev))
                                ev_avE[qt] = v_op("tensor_scalar",
                                                  out=aoev[:, qt * 64:(qt + 1) * 64],
                                                  in0=PS[qt][:, 0:64],
                                                  scalar1=colsm[:, 64:65],
                                                  scalar2=None, op0=ALU.mult, inc="V")
                            e_ao = DMA("sync",
                                       ao_d.rearrange("(qt p) o -> p qt o", p=128)
                                       [:, 4 * qc:4 * qc + 4, hname * 64:(hname + 1) * 64],
                                       aoev.rearrange("p (qt o) -> p qt o", qt=4),
                                       waits=(ev_avE[0], ev_avE[1], ev_avE[2], ev_avE[3]))
                    e_wsl_free = e_xs2_free
                e_att_done = e_ao
                if "ao" in dbg_t:
                    e_att_done = DMA("gpsimd", dbg_t["ao"][:, :], ao_d[:, :],
                                     waits=(e_att_done,))

        if phases >= 4:
            phase_sync()
            with ExitStack() as _s4:
                def SB4(name, shape, dt=F32):
                    return _s4.enter_context(nc.sbuf_tensor(name, shape, dt))
                wosl = SB4("wosl", [128, 2 * 8 * 1024], BF16)
                aot = SB4("aot", [128, 1024])
                xot = SB4("xot", [128, 1024])
                aofh = SB4("aofh", [128, 1024], BF16)
                aofl = SB4("aofl", [128, 1024], BF16)
                x1t = SB4("x1t", [128, 1024])
                x1fh = SB4("x1fh", [128, 1024], BF16)
                x1fl = SB4("x1fl", [128, 1024], BF16)
                p4a = SB4("p4a", [128, 1024])
                p4b = SB4("p4b", [128, 1024])
                mk8 = SB4("mk8", [128, 128])

                DMA("sync", wosl[:, 0:8192].rearrange("p (cc o) -> p cc o", cc=8),
                    wTd["o"]["h"].rearrange("(cc p) o -> p cc o", p=128), waits=(e_wT_done,))
                e_wo = DMA("sync", wosl[:, 8192:16384].rearrange("p (cc o) -> p cc o", cc=8),
                           wTd["o"]["l"].rearrange("(cc p) o -> p cc o", p=128),
                           waits=(e_wT_done,))
                ev_mz = v_op("memset", mk8[:, :], 0.0, inc="V")
                e_x1s = None
                e_mts = None
                ev_p4 = {}
                ev_tfree = None
                for tl in range(NTILE):
                    e_aol = DMA("sync", aot[:, :], ao_d[tl * 128:(tl + 1) * 128, :],
                                waits=(e_att_done, ev_tfree))
                    e_xol = DMA("sync", xot[:, :],
                                xq[tl * 128:(tl + 1) * 128, :],
                                waits=(ev_tfree,))
                    ev_t = None
                    for cc in range(8):
                        pst = PS[6][:, 0:128]
                        ev_t = t_tr(pst, aot[:, cc * 128:(cc + 1) * 128],
                                    waits=(e_aol, ev_p4.get(6)), inc="T")
                        ev_p4[6] = v_op("tensor_copy", out=p4a[:, cc * 128:(cc + 1) * 128],
                                        in_=pst, waits=(ev_t,), inc="V")
                    e_aoh, e_aolo = split(p4a[:, :], aofh[:, :], aofl[:, :], p4b[:, :],
                                          waits=(ev_t,))
                    for ohh in range(2):
                        bk = ohh
                        ev = None
                        for cc in range(8):
                            ev = mm3(PS[bk][:, :],
                                     aofh[:, cc * 128:(cc + 1) * 128],
                                     aofl[:, cc * 128:(cc + 1) * 128],
                                     wosl[:, cc * 1024 + ohh * 512: cc * 1024 + (ohh + 1) * 512],
                                     wosl[:, 8192 + cc * 1024 + ohh * 512: 8192 + cc * 1024 + (ohh + 1) * 512],
                                     cc == 0, cc == 7,
                                     waits=(e_aoh, e_aolo, e_wo, ev_p4.get(bk)),
                                     inc="T" if cc == 7 else None)
                        ev_p4[bk] = v_op("tensor_tensor",
                                         out=x1t[:, ohh * 512:(ohh + 1) * 512],
                                         in0=PS[bk][:, :],
                                         in1=xot[:, ohh * 512:(ohh + 1) * 512],
                                         op=ALU.add, waits=(ev, e_xol), inc="V")
                    ev_res = ev_p4[1]
                    ev_sum = v_op("tensor_reduce", out=colsm[:, 64:65], in_=x1t[:, :],
                                  axis=mybir.AxisListType.X, op=ALU.add, waits=(ev_res,),
                                  inc="V")
                    ev_ssq = a_op("activation", out=p4b[:, 0:1024], in_=x1t[:, :],
                                  func=AF.Square, accum_out=colsm[:, 65:66],
                                  waits=(ev_res,), inc="A")
                    v_op("tensor_scalar_mul", colsm[:, 64:65], colsm[:, 64:65], 1.0 / C,
                         waits=(ev_sum,))
                    v_op("tensor_scalar_mul", colsm[:, 66:67], colsm[:, 65:66], 1.0 / C,
                         waits=(ev_ssq,))
                    v_op("tensor_mul", colsm[:, 67:68], colsm[:, 64:65], colsm[:, 64:65])
                    ev = v_op("tensor_tensor", out=colsm[:, 66:67], in0=colsm[:, 66:67],
                              in1=colsm[:, 67:68], op=ALU.subtract, inc="V")
                    ev = a_op("activation", out=colsm[:, 66:67], in_=colsm[:, 66:67],
                              func=AF.Ln, bias=epsc[:, 0:1], waits=(ev,), inc="A")
                    ev = a_op("activation", out=colsm[:, 66:67], in_=colsm[:, 66:67],
                              func=AF.Exp, scale=-0.5, inc="A")
                    ev = v_op("tensor_mul", colsm[:, 67:68], colsm[:, 64:65], colsm[:, 66:67],
                              waits=(ev,))
                    ev_nmr = v_op("tensor_scalar_mul", colsm[:, 67:68], colsm[:, 67:68], -1.0,
                                  inc="V")
                    ev_x1 = v_op("tensor_scalar", out=x1t[:, :], in0=x1t[:, :],
                                 scalar1=colsm[:, 66:67], scalar2=colsm[:, 67:68],
                                 op0=ALU.mult, op1=ALU.add, waits=(ev_nmr,), inc="V")
                    e_x1s = DMA("sync", x1_buf[tl * 128:(tl + 1) * 128, :], x1t[:, :],
                                waits=(ev_x1,))
                    ev_t2 = None
                    for cc in range(8):
                        pst = PS[6][:, 0:128]
                        ev_t2 = t_tr(pst, x1t[:, cc * 128:(cc + 1) * 128],
                                     waits=(ev_x1, ev_p4.get(6)), inc="T")
                        ev_p4[6] = v_op("tensor_copy", out=p4a[:, cc * 128:(cc + 1) * 128],
                                        in_=pst, waits=(ev_t2,), inc="V")
                    e_xh, e_xl = split(p4a[:, :], x1fh[:, :], x1fl[:, :], p4b[:, :],
                                       waits=(ev_t2,))
                    ev = None
                    for cc in range(8):
                        ev = mm3(PS[7][:, 0:E],
                                 x1fh[:, cc * 128:(cc + 1) * 128],
                                 x1fl[:, cc * 128:(cc + 1) * 128],
                                 rwsl[:, cc * 8:(cc + 1) * 8],
                                 rwsl[:, 64 + cc * 8:64 + (cc + 1) * 8],
                                 cc == 0, cc == 7,
                                 waits=(e_xh, e_xl, e_rw, ev_p4.get(7)),
                                 inc="T" if cc == 7 else None)
                    gsl = slice(tl * E, (tl + 1) * E)
                    ev_lg = v_op("tensor_tensor", out=gates_all[:, gsl], in0=PS[7][:, 0:E],
                                 in1=rbb[:, :], op=ALU.add, waits=(ev, e_rbb), inc="V")
                    ev_p4[7] = ev_lg
                    v_op("tensor_reduce", out=colsm[:, 64:65], in_=gates_all[:, gsl],
                         axis=mybir.AxisListType.X, op=ALU.max)
                    ev = v_op("tensor_scalar_mul", colsm[:, 64:65], colsm[:, 64:65], -1.0,
                              inc="V")
                    ev = a_op("activation", out=gates_all[:, gsl], in_=gates_all[:, gsl],
                              func=AF.Exp, bias=colsm[:, 64:65],
                              accum_out=colsm[:, 65:66], waits=(ev,), inc="A")
                    v_op("reciprocal", out=colsm[:, 65:66], in_=colsm[:, 65:66], waits=(ev,))
                    ev_g = v_op("tensor_scalar", out=gates_all[:, gsl],
                                in0=gates_all[:, gsl], scalar1=colsm[:, 65:66],
                                scalar2=None, op0=ALU.mult, inc="V")
                    v_op("tensor_reduce", out=colsm[:, 64:65], in_=gates_all[:, gsl],
                         axis=mybir.AxisListType.X, op=ALU.max)
                    v_op("tensor_scalar", out=p4a[:, 0:E], in0=gates_all[:, gsl],
                         scalar1=colsm[:, 64:65], scalar2=None, op0=ALU.is_equal)
                    v_op("tensor_tensor", out=p4a[:, 0:E], in0=p4a[:, 0:E], in1=cw8[:, :],
                         op=ALU.mult)
                    v_op("tensor_reduce", out=colsm[:, 64:65], in_=p4a[:, 0:E],
                         axis=mybir.AxisListType.X, op=ALU.max)
                    v_op("tensor_scalar", out=oh1_all[:, gsl], in0=p4a[:, 0:E],
                         scalar1=colsm[:, 64:65], scalar2=None, op0=ALU.is_equal)
                    v_op("scalar_tensor_tensor", out=p4a[:, 0:E], in0=oh1_all[:, gsl],
                         scalar=-2.0, in1=gates_all[:, gsl], op0=ALU.mult, op1=ALU.add)
                    v_op("tensor_reduce", out=colsm[:, 64:65], in_=p4a[:, 0:E],
                         axis=mybir.AxisListType.X, op=ALU.max)
                    v_op("tensor_scalar", out=p4b[:, 0:E], in0=p4a[:, 0:E],
                         scalar1=colsm[:, 64:65], scalar2=None, op0=ALU.is_equal)
                    v_op("tensor_tensor", out=p4b[:, 0:E], in0=p4b[:, 0:E], in1=cw8[:, :],
                         op=ALU.mult)
                    v_op("tensor_reduce", out=colsm[:, 64:65], in_=p4b[:, 0:E],
                         axis=mybir.AxisListType.X, op=ALU.max)
                    v_op("tensor_scalar", out=oh2_all[:, gsl], in0=p4b[:, 0:E],
                         scalar1=colsm[:, 64:65], scalar2=None, op0=ALU.is_equal)
                    ev_m = v_op("tensor_tensor", out=mk8[:, 0:E], in0=oh1_all[:, gsl],
                                in1=oh2_all[:, gsl], op=ALU.add,
                                waits=(ev_mz, ev_p4.get("mktr")), inc="V")
                    evt = t_tr(PS[6][:, 0:128], mk8[:, :], waits=(ev_m, ev_p4.get(6)),
                               inc="T")
                    ev_p4["mktr"] = evt
                    ev_p4[6] = v_op("tensor_copy", out=p4a[0:E, 0:128], in_=PS[6][0:E, 0:128],
                                    waits=(evt,), inc="V")
                    ev_tfree = ev_p4[6]
                    e_mts = DMA("sync", mT_buf[:, tl * 128:(tl + 1) * 128], p4a[0:E, 0:128],
                                waits=(ev_p4[6],))
                e_x1_done, e_mt_done = e_x1s, e_mts
                if "x1" in dbg_t:
                    e_x1_done = DMA("gpsimd", dbg_t["x1"][:, :], x1_buf[:, :],
                                    waits=(e_x1_done,))
                if "mask" in dbg_t:
                    e_mt_done = DMA("gpsimd", dbg_t["mask"][:, :], mT_buf[:, :],
                                    waits=(e_mt_done,))
                if "gates" in dbg_t:
                    EMIT("gpsimd", lambda e: e.dma_start(out=dbg_t["gates"][:, :],
                                                         in_=gates_all[:, :]),
                         waits=(ev_g,), inc="dG", inc_by=16)

        if phases >= 5:
            phase_sync()
            with ExitStack() as _s5:
                def SB5(name, shape, dt=F32):
                    return _s5.enter_context(nc.sbuf_tensor(name, shape, dt))
                msk8 = SB5("msk8", [128, 2048])
                cum8 = SB5("cum8", [128, 2048])
                ones8 = SB5("ones8", [128, 2048])
                carry = SB5("carry", [128, 1])
                cinb = SB5("cinb", [128, 128])
                cblk = SB5("cblk", [128, 128])
                p5a = SB5("p5a", [128, 128])
                p5c = SB5("p5c", [128, 128])
                def cc_fn(e, ins_, outs_):
                    return e.collective_compute(
                        "AllGather", ALU.bypass, replica_groups=[list(range(NCORE))],
                        ins=ins_, outs=outs_)
                e_cc1 = EMIT("gpsimd",
                             lambda e: cc_fn(e, [x1_buf[:, :]], [x1_all[:, :]]),
                             waits=(e_x1_done,), inc="CC")
                e_cc2 = EMIT("gpsimd",
                             lambda e: cc_fn(e, [mT_buf[:, :]], [mT_all[:, :]]),
                             waits=(e_mt_done,), inc="CC")
                v_op("memset", ones8[0:8, :], 1.0)
                ev_cr = v_op("memset", carry[0:8, :], 0.0, inc="V")
                e_cumd = None
                for ck in range(NCORE):
                    e_ml = DMA("sync", msk8[0:8, :], mT_all[ck * E:(ck + 1) * E, :],
                               waits=(e_cc2, e_cumd))
                    ev_sc = v_op("tensor_tensor_scan", out=cum8[0:8, :], data0=ones8[0:8, :],
                                 data1=msk8[0:8, :], initial=carry[0:8, 0:1],
                                 op0=ALU.mult, op1=ALU.add, waits=(e_ml,), inc="V")
                    ev_cr = v_op("tensor_copy", out=carry[0:8, :], in_=cum8[0:8, 2047:2048],
                                 inc="V")
                    e_cumd = DMA("sync", cum_d[:, ck * 2048:(ck + 1) * 2048], cum8[0:8, :],
                                 waits=(ev_cr,))
                e_cum_done = e_cumd
                if "cum" in dbg_t:
                    e_cum_done = DMA("gpsimd", dbg_t["cum"][:, :], cum_d[:, :],
                                     waits=(e_cum_done,))
                g_r = core_reg("gpsimd")
                e_cin = DMA("gpsimd", p5c[0:1, :],
                            cum_d[bass.ds(g_r, 1), 127::128], waits=(e_cum_done,),
                            slow=True)
                evb = t_mm(PS[6][:, 0:128], onesr[0:1, :], p5c[0:1, :], True, True,
                           waits=(e_cin,), inc="T")
                ev_cinb = v_op("tensor_copy", out=cinb[:, :], in_=PS[6][:, 0:128],
                               waits=(evb,), inc="V")
                ev_idx = None
                for st_ in range(NCAPT):
                    v_op("tensor_scalar_add", colsm[:, 68:69], i128[:, :], float(st_ * 128),
                         waits=(ev_cinb,))
                    v_op("tensor_scalar", out=p5a[:, :], in0=cinb[:, :],
                         scalar1=colsm[:, 68:69], scalar2=None, op0=ALU.is_le)
                    v_op("tensor_reduce", out=colsm[:, 69:70], in_=p5a[:, :],
                         axis=mybir.AxisListType.X, op=ALU.add)
                    v_op("tensor_scalar_min", colsm[:, 69:70], colsm[:, 69:70], 127.0)
                    ev = v_op("tensor_tensor", out=colsm[:, 70:71], in0=colsm[:, 69:70],
                              in1=meta_sb[:, 0:1], op=ALU.add, inc="V")
                    ev_i = v_op("tensor_copy", out=idx_sb[:, st_:st_ + 1],
                                in_=colsm[:, 70:71], waits=(ev,), inc="V")
                    e_gb = g_opx("indirect_dma_start",
                                 out=cblk[:, :], out_offset=None,
                                 in_=cum_d.rearrange("e (b t) -> (e b) t", t=128),
                                 in_offset=IndirectOffsetOnAxis(ap=idx_sb[:, st_:st_ + 1], axis=0),
                                 waits=(ev_i, e_cum_done), inc="dG", inc_by=16)
                    v_op("tensor_scalar", out=cblk[:, :], in0=cblk[:, :],
                         scalar1=colsm[:, 68:69], scalar2=None, op0=ALU.is_le,
                         waits=(e_gb,))
                    v_op("tensor_reduce", out=colsm[:, 71:72], in_=cblk[:, :],
                         axis=mybir.AxisListType.X, op=ALU.add)
                    v_op("tensor_scalar_mul", colsm[:, 69:70], colsm[:, 69:70], 128.0)
                    v_op("tensor_add", colsm[:, 71:72], colsm[:, 71:72], colsm[:, 69:70])
                    ev = v_op("tensor_scalar_min", colsm[:, 71:72], colsm[:, 71:72],
                              float(NTOT - 1), inc="V")
                    ev_idx = v_op("tensor_copy", out=idx_sb[:, st_:st_ + 1],
                                  in_=colsm[:, 71:72], waits=(ev,), inc="V")
                e_idx_done = ev_idx
                if "idx" in dbg_t:
                    ev_dbg = v_op("tensor_copy", out=p5a[:, 0:20], in_=idx_sb[:, :],
                                  waits=(e_idx_done,), inc="V")
                    EMIT("gpsimd", lambda e: e.dma_start(out=dbg_t["idx"][:, :],
                                                         in_=p5a[:, 0:20]),
                         waits=(ev_dbg,), inc="dG", inc_by=16)

        if phases >= 6:
            phase_sync()
            with ExitStack() as _s6:
                def SB6(name, shape, dt=F32):
                    return _s6.enter_context(nc.sbuf_tensor(name, shape, dt))
                xes = SB6("xes", [128, 1024])
                xefh = SB6("xefh", [128, 8 * 512], BF16)
                xefl = SB6("xefl", [128, 8 * 512], BF16)
                hhi = SB6("hhi", [128, 32 * 512], BF16)
                hlo = SB6("hlo", [128, 32 * 512], BF16)
                w1f = SB6("w1f", [128, 2 * 1024])
                w1h = SB6("w1h", [128, 2 * 1024], BF16)
                w1l = SB6("w1l", [128, 2 * 1024], BF16)
                w2f = SB6("w2f", [128, 2 * 1024])
                w2h = SB6("w2h", [128, 2 * 1024], BF16)
                w2l = SB6("w2l", [128, 2 * 1024], BF16)
                xet_h = SB6("xet_h", [128, 1024], BF16)
                xet_l = SB6("xet_l", [128, 1024], BF16)
                hst = SB6("hst", [128, 512])
                o2st = SB6("o2st", [128, 1024])
                b1c = SB6("b1c", [128, 32])
                b2c = SB6("b2c", [128, 8])
                p6a = SB6("p6a", [128, 1024])
                p6b = SB6("p6b", [128, 2048])
                e_b1 = DMA("sync", b1c[:, :],
                           b1_i.rearrange("z (m p) -> (z p) m", p=128), slow=True)
                e_b2 = DMA("sync", b2c[:, :],
                           b2_i.rearrange("z (m p) -> (z p) m", p=128), slow=True)
                ev_split_free = None
                e_o2 = None
                ev_p6 = {}
                e_xe_free = None
                for cap_c in range(5):
                    ev_xe = None
                    for j in range(4):
                        st_ = cap_c * 4 + j
                        e_g = g_opx("indirect_dma_start",
                                    out=xes[:, :], out_offset=None,
                                    in_=x1_all[:, :],
                                    in_offset=IndirectOffsetOnAxis(ap=idx_sb[:, st_:st_ + 1], axis=0),
                                    waits=(e_idx_done, e_cc1, e_xe_free,
                                           ev_p6.get(("w1mm", 0)), ev_p6.get(("w1mm", 1))),
                                    inc="dG", inc_by=16)
                        ev_t = None
                        for cc in range(8):
                            pst = PS[6][:, 0:128]
                            ev_t = t_tr(pst, xes[:, cc * 128:(cc + 1) * 128],
                                        waits=(e_g, ev_p6.get(6)), inc="T")
                            ev_p6[6] = v_op("tensor_copy",
                                            out=p6a[:, cc * 128:(cc + 1) * 128],
                                            in_=pst,
                                            waits=(ev_t, ev_p6.get("p6a_free")), inc="V")
                        e_xe_free = ev_t
                        if "xed" in dbg_t and cap_c == 0:
                            DMA("gpsimd", dbg_t["xed"][j * 128:(j + 1) * 128, :],
                                xes[:, :], waits=(e_g,))
                        ev1 = v_op("tensor_copy", out=xet_h[:, :], in_=p6a[:, :],
                                   waits=(ev_p6.get(6),), inc="V")
                        v_op("tensor_tensor", out=p6b[:, 0:1024], in0=p6a[:, :],
                             in1=xet_h[:, :], op=ALU.subtract)
                        v_op("tensor_copy", out=xet_l[:, :], in_=p6b[:, 0:1024])
                        v_op("tensor_copy",
                             out=xefh.rearrange("p (cc t) -> p cc t", cc=8)[:, :, j * 128:(j + 1) * 128],
                             in_=xet_h.rearrange("p (cc t) -> p cc t", cc=8))
                        ev_xe = v_op("tensor_copy",
                                     out=xefl.rearrange("p (cc t) -> p cc t", cc=8)[:, :, j * 128:(j + 1) * 128],
                                     in_=xet_l.rearrange("p (cc t) -> p cc t", cc=8),
                                     inc="V")
                    ev_h = None
                    e_w1 = {}
                    for m in range(32):
                        pb = m % 2
                        e_w1[pb] = DMA("sync", w1f[:, pb * 1024:(pb + 1) * 1024].rearrange("p (cc hh) -> p cc hh", cc=8),
                                       w1_i.rearrange("(cc p) hh -> p cc hh", p=128)[:, :, m * 128:(m + 1) * 128],
                                       waits=(ev_p6.get(("w1s", pb)),))
                        eh, el = split(w1f[:, pb * 1024:(pb + 1) * 1024],
                                       w1h[:, pb * 1024:(pb + 1) * 1024],
                                       w1l[:, pb * 1024:(pb + 1) * 1024],
                                       p6b[:, 0:1024],
                                       waits=(e_w1[pb], ev_p6.get(("w1mm", pb))))
                        ev_p6[("w1s", pb)] = el
                        bk = 4 + (m % 2)
                        ev = None
                        for cc in range(8):
                            ev = mm3(PS[bk][:, :],
                                     w1h[:, pb * 1024 + cc * 128: pb * 1024 + (cc + 1) * 128],
                                     w1l[:, pb * 1024 + cc * 128: pb * 1024 + (cc + 1) * 128],
                                     xefh[:, cc * 512:(cc + 1) * 512],
                                     xefl[:, cc * 512:(cc + 1) * 512],
                                     cc == 0, cc == 7,
                                     waits=(ev_xe, el, ev_p6.get(bk)),
                                     inc="T" if cc == 7 else None)
                        ev_p6[("w1mm", pb)] = ev
                        ev_gl = a_op("activation", out=hst[:, :], in_=PS[bk][:, :],
                                     func=AF.Gelu, bias=b1c[:, m:m + 1],
                                     waits=(ev, e_b1, ev_p6.get("hdmp")), inc="A")
                        ev_p6[bk] = ev_gl
                        if "hd" in dbg_t and cap_c == 0:
                            ev_p6["hdmp"] = DMA("gpsimd",
                                                dbg_t["hd"][m * 128:(m + 1) * 128, :],
                                                hst[:, :], waits=(ev_gl,))
                        hsl = slice(m * 512, (m + 1) * 512)
                        _, ev_h = split(hst[:, :], hhi[:, hsl], hlo[:, hsl], p6b[:, 0:512],
                                        waits=(ev_gl, ev_split_free))
                    for dgrp in range(2):
                        ev_o2e = {}
                        for hk in range(32):
                            pb = hk % 2
                            e_w2 = DMA("sync", w2f[:, pb * 1024:(pb + 1) * 1024],
                                       w2_i[hk * 128:(hk + 1) * 128, :],
                                       waits=(ev_p6.get(("w2s", pb)),))
                            eh, el = split(w2f[:, pb * 1024:(pb + 1) * 1024],
                                           w2h[:, pb * 1024:(pb + 1) * 1024],
                                           w2l[:, pb * 1024:(pb + 1) * 1024],
                                           p6b[:, 0:1024],
                                           waits=(e_w2, ev_p6.get(("w2mm", pb))))
                            ev_p6[("w2s", pb)] = el
                            for dd in range(4):
                                d = dgrp * 4 + dd
                                ev = mm3(PS[dd][:, :],
                                         w2h[:, pb * 1024 + d * 128: pb * 1024 + (d + 1) * 128],
                                         w2l[:, pb * 1024 + d * 128: pb * 1024 + (d + 1) * 128],
                                         hhi[:, hk * 512:(hk + 1) * 512],
                                         hlo[:, hk * 512:(hk + 1) * 512],
                                         hk == 0, hk == 31,
                                         waits=(ev_h, el,
                                                ev_p6.get(dd) if hk == 0 else None),
                                         inc="T" if (hk == 31 or dd == 3) else None)
                                if hk == 31:
                                    ev_o2e[dd] = ev
                            ev_p6[("w2mm", pb)] = ev
                        for dd in range(4):
                            d = dgrp * 4 + dd
                            ev_cp = v_op("tensor_scalar", out=o2st[:, 0:512], in0=PS[dd][:, :],
                                         scalar1=b2c[:, d:d + 1], scalar2=None,
                                         op0=ALU.add,
                                         waits=(ev_o2e[dd], e_b2, ev_p6.get("o2st")), inc="V")
                            ev_p6[dd] = ev_cp
                            for j in range(4):
                                pst = PS[7][:, 0:128]
                                evt = t_tr(pst, o2st[:, j * 128:(j + 1) * 128],
                                           waits=(ev_cp, ev_p6.get(7)), inc="T")
                                ev_e = v_op("tensor_copy",
                                            out=p6a[:, j * 128:(j + 1) * 128],
                                            in_=pst,
                                            waits=(evt, ev_p6.get("p6a_free")), inc="V")
                                ev_p6[7] = ev_e
                            ev_p6["o2st"] = ev_e
                            e_o2 = DMA("sync",
                                       out2_b.rearrange("(j p) o -> p j o", p=128)
                                       [:, cap_c * 4: cap_c * 4 + 4, d * 128:(d + 1) * 128],
                                       p6a[:, 0:512].rearrange("p (j t) -> p j t", j=4),
                                       waits=(ev_e,))
                            ev_p6["p6a_free"] = e_o2
                    ev_split_free = ev
                e_ffn_done = e_o2
                if "out2" in dbg_t:
                    e_ffn_done = DMA("gpsimd", dbg_t["out2"][:, :], out2_b[:, :],
                                     waits=(e_ffn_done,))

        class _P7Stop(Exception):
            pass
        if phases >= 7:
            phase_sync()
            with ExitStack() as _s7:
                def SB7(name, shape, dt=F32):
                    return _s7.enter_context(nc.sbuf_tensor(name, shape, dt))
                cumo = SB7("cumo", [128, 2048])
                msko = SB7("msko", [128, 2048])
                posg8 = SB7("posg8", [128, 2048])
                vld8 = SB7("vld8", [128, 2048])
                posgt = SB7("posgt", [128, NTILE * E])
                vldt = SB7("vldt", [128, NTILE * E])
                ridx = SB7("ridx", [128, 2], I32)
                g1 = SB7("g1", [128, 1024])
                g2 = SB7("g2", [128, 1024])
                x1o = SB7("x1o", [128, 1024])
                yt = SB7("yt", [128, 1024])
                rdbg = SB7("rdbg", [128, 2 * NTILE])
                p7a = SB7("p7a", [128, 1024])
                try:
                    if P7SUB < 1:
                        raise _P7Stop()
                    def cc_fn2(e):
                        return e.collective_compute(
                            "AllGather", ALU.bypass, replica_groups=[list(range(NCORE))],
                            ins=[out2_b[:, :]], outs=[out2_all[:, :]])
                    e_cc3 = EMIT("gpsimd", cc_fn2, waits=(e_ffn_done,), inc="CC")

                    if P7SUB < 2:
                        raise _P7Stop()
                    g_g = core_reg("gpsimd")
                    tokb = g_g * TOK
                    e_cl = DMA("gpsimd", cumo[0:8, :], cum_d[:, bass.ds(tokb, TOK)],
                               waits=(e_cum_done,))
                    e_ml = DMA("sync", msko[0:8, :], mT_buf[:, :], waits=(e_mt_done,))
                    v_op("tensor_scalar_add", posg8[0:8, :], cumo[0:8, :], -1.0, waits=(e_cl,))
                    v_op("tensor_scalar", out=vld8[0:8, :], in0=posg8[0:8, :], scalar1=float(CAP),
                         scalar2=None, op0=ALU.is_lt)
                    ev_vd = v_op("tensor_tensor", out=vld8[0:8, :], in0=vld8[0:8, :], in1=msko[0:8, :],
                                 op=ALU.mult, waits=(e_ml,), inc="V")
                    v_op("tensor_scalar_add", posg8[0:8, :], posg8[0:8, :], float(-(CAP - 1)))
                    v_op("tensor_tensor", out=posg8[0:8, :], in0=posg8[0:8, :], in1=vld8[0:8, :],
                         op=ALU.mult)
                    ev_pg = v_op("tensor_scalar_add", posg8[0:8, :], posg8[0:8, :], float(CAP - 1),
                                 inc="V")
                    ev_z = v_op("memset", p7a[:, 0:128], 0.0, inc="V")
                    ev_p7 = {}
                    for tl in range(NTILE):
                        ev1 = v_op("tensor_copy", out=p7a[0:8, 0:128],
                                   in_=posg8[0:8, tl * 128:(tl + 1) * 128],
                                   waits=(ev_pg, ev_z, ev_p7.get("tr")), inc="V")
                        evt = t_tr(PS[6][:, 0:128], p7a[:, 0:128], waits=(ev1, ev_p7.get(6)),
                                   inc="T")
                        ev_p7[6] = v_op("tensor_copy", out=posgt[:, tl * E:(tl + 1) * E],
                                        in_=PS[6][:, 0:E], waits=(evt,), inc="V")
                        ev2 = v_op("tensor_copy", out=p7a[0:8, 0:128],
                                   in_=vld8[0:8, tl * 128:(tl + 1) * 128],
                                   waits=(ev_vd, ev_p7[6]), inc="V")
                        evt = t_tr(PS[6][:, 0:128], p7a[:, 0:128], waits=(ev2,), inc="T")
                        ev_p7["tr"] = v_op("tensor_copy", out=vldt[:, tl * E:(tl + 1) * E],
                                           in_=PS[6][:, 0:E], waits=(evt,), inc="V")
                    ev_tp_done = ev_p7["tr"]

                    e_out = None
                    if P7SUB < 3:
                        raise _P7Stop()
                    for tl in range(NTILE):
                        gsl = slice(tl * E, (tl + 1) * E)
                        v_op("tensor_tensor", out=p7a[:, 0:E], in0=posgt[:, gsl], in1=ec8[:, :],
                             op=ALU.add, waits=(ev_tp_done, e_ec8, ev_p7.get("yt")))
                        v_op("tensor_tensor", out=p7a[:, 8:8 + E], in0=p7a[:, 0:E],
                             in1=oh1_all[:, gsl], op=ALU.mult)
                        v_op("tensor_reduce", out=p7a[:, 16:17], in_=p7a[:, 8:8 + E],
                             axis=mybir.AxisListType.X, op=ALU.add)
                        v_op("tensor_tensor", out=p7a[:, 8:8 + E], in0=p7a[:, 0:E],
                             in1=oh2_all[:, gsl], op=ALU.mult)
                        v_op("tensor_reduce", out=p7a[:, 17:18], in_=p7a[:, 8:8 + E],
                             axis=mybir.AxisListType.X, op=ALU.add)
                        if "ridx" in dbg_t:
                            v_op("tensor_copy", out=rdbg[:, 2 * tl:2 * tl + 2],
                                 in_=p7a[:, 16:18])
                        if DUMMY_GATHER:
                            v_op("memset", p7a[:, 16:18], 0.0)
                        ev_r = v_op("tensor_copy", out=ridx[:, :], in_=p7a[:, 16:18], inc="V")
                        v_op("tensor_tensor", out=p7a[:, 0:E], in0=gates_all[:, gsl],
                             in1=vldt[:, gsl], op=ALU.mult)
                        v_op("tensor_tensor", out=p7a[:, 8:8 + E], in0=p7a[:, 0:E],
                             in1=oh1_all[:, gsl], op=ALU.mult)
                        v_op("tensor_reduce", out=p7a[:, 16:17], in_=p7a[:, 8:8 + E],
                             axis=mybir.AxisListType.X, op=ALU.add)
                        v_op("tensor_tensor", out=p7a[:, 8:8 + E], in0=p7a[:, 0:E],
                             in1=oh2_all[:, gsl], op=ALU.mult)
                        ev_w = v_op("tensor_reduce", out=p7a[:, 17:18], in_=p7a[:, 8:8 + E],
                                    axis=mybir.AxisListType.X, op=ALU.add, inc="V")
                        e_g1 = g_opx("indirect_dma_start",
                                     out=g1[:, :], out_offset=None, in_=out2_all[:, :],
                                     in_offset=IndirectOffsetOnAxis(ap=ridx[:, 0:1], axis=0),
                                     waits=(ev_r, e_cc3), inc="dG", inc_by=16)
                        e_g2 = g_opx("indirect_dma_start",
                                     out=g2[:, :], out_offset=None, in_=out2_all[:, :],
                                     in_offset=IndirectOffsetOnAxis(ap=ridx[:, 1:2], axis=0),
                                     waits=(ev_r,), inc="dG", inc_by=16)
                        e_x1l = DMA("sync", x1o[:, :], x1_buf[tl * 128:(tl + 1) * 128, :],
                                    waits=(e_x1_done, ev_p7.get("x1o")))
                        v_op("tensor_scalar", out=yt[:, :], in0=g1[:, :], scalar1=p7a[:, 16:17],
                             scalar2=None, op0=ALU.mult, waits=(e_g1, ev_w, ev_p7.get("yt")))
                        v_op("tensor_scalar", out=g2[:, :], in0=g2[:, :], scalar1=p7a[:, 17:18],
                             scalar2=None, op0=ALU.mult, waits=(e_g2,))
                        v_op("tensor_add", yt[:, :], yt[:, :], g2[:, :])
                        ev_y = v_op("tensor_tensor", out=yt[:, :], in0=yt[:, :], in1=x1o[:, :],
                                    op=ALU.add, waits=(e_x1l,), inc="V")
                        ev_p7["x1o"] = ev_y
                        ev_sum = v_op("tensor_reduce", out=colsm[:, 64:65], in_=yt[:, :],
                                      axis=mybir.AxisListType.X, op=ALU.add, inc="V")
                        ev_ssq = a_op("activation", out=p7a[:, 0:1024], in_=yt[:, :],
                                      func=AF.Square, accum_out=colsm[:, 65:66],
                                      waits=(ev_y,), inc="A")
                        v_op("tensor_scalar_mul", colsm[:, 64:65], colsm[:, 64:65], 1.0 / C,
                             waits=(ev_sum,))
                        v_op("tensor_scalar_mul", colsm[:, 66:67], colsm[:, 65:66], 1.0 / C,
                             waits=(ev_ssq,))
                        v_op("tensor_mul", colsm[:, 67:68], colsm[:, 64:65], colsm[:, 64:65])
                        ev = v_op("tensor_tensor", out=colsm[:, 66:67], in0=colsm[:, 66:67],
                                  in1=colsm[:, 67:68], op=ALU.subtract, inc="V")
                        ev = a_op("activation", out=colsm[:, 66:67], in_=colsm[:, 66:67],
                                  func=AF.Ln, bias=epsc[:, 0:1], waits=(ev,), inc="A")
                        ev = a_op("activation", out=colsm[:, 66:67], in_=colsm[:, 66:67],
                                  func=AF.Exp, scale=-0.5, inc="A")
                        ev = v_op("tensor_mul", colsm[:, 67:68], colsm[:, 64:65],
                                  colsm[:, 66:67], waits=(ev,))
                        ev_nmr = v_op("tensor_scalar_mul", colsm[:, 67:68], colsm[:, 67:68],
                                      -1.0, inc="V")
                        ev_o = v_op("tensor_scalar", out=yt[:, :], in0=yt[:, :],
                                    scalar1=colsm[:, 66:67], scalar2=colsm[:, 67:68],
                                    op0=ALU.mult, op1=ALU.add, waits=(ev_nmr,), inc="V")
                        ev_p7["yt"] = ev_o
                        e_out = DMA("sync", out_t[tl * 128:(tl + 1) * 128, :], yt[:, :],
                                    waits=(ev_o,))
                    if "ridx" in dbg_t:
                        DMA("gpsimd", dbg_t["ridx"][:, :], rdbg[:, :], waits=(ev_o,))


                except _P7Stop:
                    pass
        final_waits = []
        for sname in ("dS", "dG", "dA", "CC"):
            if sch.semval.get(sname):
                final_waits.append((sname, sch.semval[sname], "gpsimd"))
        EMIT("gpsimd", lambda e: e.nop(), waits=tuple(final_waits))

        sch.replay(block)
    return nc


def kernel(L0_attn_x, L0_attn_wq, L0_attn_wk, L0_attn_wv, L0_attn_wo,
           L0_attn_pope_bias, L0_ffn_router_w, L0_ffn_router_b,
           L0_ffn_w1, L0_ffn_b1, L0_ffn_w2, L0_ffn_b2, dbg=(), phases=99,
           DUMMY_GATHER=False):
    x = np.ascontiguousarray(np.asarray(L0_attn_x, dtype=np.float32))
    nc = build_nc(dbg=dbg, phases=phases, DUMMY_GATHER=DUMMY_GATHER, P7SUB=9)
    in_maps = []
    for c in range(NCORE):
        b = c // 2
        mcol = np.zeros((128, 4), dtype=np.float32)
        mcol[:, 0] = c * 128.0
        in_maps.append({
            "xb": x[b],
            "xq": x[b, (c % 2) * TOK:(c % 2) * TOK + TOK],
            "wq": np.asarray(L0_attn_wq, dtype=np.float32),
            "wk": np.asarray(L0_attn_wk, dtype=np.float32),
            "wv": np.asarray(L0_attn_wv, dtype=np.float32),
            "wo": np.asarray(L0_attn_wo, dtype=np.float32),
            "pope": np.asarray(L0_attn_pope_bias, dtype=np.float32),
            "rw": np.asarray(L0_ffn_router_w, dtype=np.float32),
            "rbv": np.asarray(L0_ffn_router_b, dtype=np.float32).reshape(1, E),
            "w1e": np.asarray(L0_ffn_w1[c], dtype=np.float32),
            "b1e": np.asarray(L0_ffn_b1[c], dtype=np.float32).reshape(1, DHID),
            "w2e": np.asarray(L0_ffn_w2[c], dtype=np.float32),
            "b2e": np.asarray(L0_ffn_b2[c], dtype=np.float32).reshape(1, C),
            "meta": mcol,
        })
    res = run_bass_kernel_spmd(nc, in_maps, core_ids=list(range(NCORE)))
    outs = [np.asarray(res.results[c]["out"]) for c in range(NCORE)]
    full = np.concatenate(outs, axis=0).reshape(B, S, C)
    if dbg:
        return full, res
    return full


# version 03
# speedup vs baseline: 1.0017x; 1.0017x over previous
import os
import numpy as np
from contextlib import ExitStack

import concourse.bass as bass
import concourse.mybir as mybir
from concourse.bass_core_rust import IndirectOffsetOnAxis
from concourse.bass_utils import run_bass_kernel_spmd

F32 = mybir.dt.float32
BF16 = mybir.dt.bfloat16
I32 = mybir.dt.int32
AF = mybir.ActivationFunctionType
ALU = mybir.AluOpType

NCORE = 8
B, S, C, H, Dh = 4, 4096, 1024, 16, 64
TOK = 2048
NTILE = TOK // 128
E, CAP, DHID = 8, 2560, 4096
NTOT = B * S
NCAPT = CAP // 128

_L = np.float32(np.log(10000.0))
LN1E4_HI = float(_L)
LN1E4_LO = float(np.float32(np.log(10000.0) - np.float64(_L)))
_P = np.float32(np.pi / 2)
PIO2_HI = float(_P)
PIO2_LO = float(np.float32(np.pi / 2 - np.float64(_P)))
TWOPI = float(np.float32(2 * np.pi))

ENGS = ("sync", "tensor", "vector", "scalar", "gpsimd")
_E2T = {"sync": mybir.EngineType.SP, "tensor": mybir.EngineType.PE,
        "vector": mybir.EngineType.DVE, "scalar": mybir.EngineType.Activation,
        "gpsimd": mybir.EngineType.Pool}


class Sched:
    def __init__(self, nc):
        self.nc = nc
        self.ops = {e: [] for e in ENGS}
        self.semval = {}
        self.sems = {}
        self.maxwait = {}

    def emit(self, eng, fn, waits=(), inc=None, inc_by=1, post_drain=False):
        w = []
        for ev in waits:
            if ev is None:
                continue
            sname, val, peng = ev
            if peng == eng and not (sname.startswith("d") or sname == "CC"):
                continue
            key = (eng, sname)
            if self.maxwait.get(key, -1) >= val:
                continue
            self.maxwait[key] = val
            w.append((sname, val))
        self.ops[eng].append((fn, w, inc, inc_by, post_drain))
        if inc is not None:
            self.semval[inc] = self.semval.get(inc, 0) + inc_by
            return (inc, self.semval[inc], eng)
        return None

    def dma(self, eng, out, in_, waits=(), tag=None, slow=False):
        sem = {"sync": "dS", "gpsimd": "dG", "scalar": "dA"}[eng]

        def fn(e, out=out, in_=in_, slow=slow):
            if slow:
                return e.dma_start(out=out, in_=in_, allow_slow_non_contiguous=True)
            return e.dma_start(out=out, in_=in_)
        fn._tag = tag or f"dma {getattr(out.tensor, 'name', '?')}<-{getattr(in_.tensor, 'name', '?')}"
        return self.emit(eng, fn, waits, inc=sem, inc_by=16)

    def replay(self, block):
        for eng in ENGS:
            if not self.ops[eng]:
                continue

            def make(body_eng):
                def body(e):
                    for _oi, (fn, w, inc, by, pdr) in enumerate(self.ops[body_eng]):
                        for sname, val in w:
                            e.wait_ge(self.sems[sname], val)
                        try:
                            inst = fn(e)
                            if pdr:
                                inst = e.drain()
                        except Exception as ex:
                            import traceback
                            traceback.print_exc()
                            raise RuntimeError(
                                f"replay fail eng={body_eng} op#{_oi} tag={getattr(fn, '_tag', None)}: {ex}") from ex
                        if inc is not None and inst is not None:
                            inst.then_inc(self.sems[inc], by)
                return body
            getattr(block, eng)(make(eng))


def build_nc(dbg=(), phases=99, DUMMY_GATHER=False, P7SUB=9):
    nc = bass.Bass("TRN2", target_bir_lowering=False, detect_race_conditions=False)

    xb = nc.dram_tensor("xb", [S, C], F32, kind="ExternalInput").ap()
    xq = nc.dram_tensor("xq", [TOK, C], F32, kind="ExternalInput").ap()
    wq_i = nc.dram_tensor("wq", [C, C], F32, kind="ExternalInput").ap()
    wk_i = nc.dram_tensor("wk", [C, C], F32, kind="ExternalInput").ap()
    wv_i = nc.dram_tensor("wv", [C, C], F32, kind="ExternalInput").ap()
    wo_i = nc.dram_tensor("wo", [C, C], F32, kind="ExternalInput").ap()
    pope = nc.dram_tensor("pope", [H, Dh], F32, kind="ExternalInput").ap()
    rw_i = nc.dram_tensor("rw", [C, E], F32, kind="ExternalInput").ap()
    rb_i = nc.dram_tensor("rbv", [1, E], F32, kind="ExternalInput").ap()
    w1_i = nc.dram_tensor("w1e", [C, DHID], F32, kind="ExternalInput").ap()
    b1_i = nc.dram_tensor("b1e", [1, DHID], F32, kind="ExternalInput").ap()
    w2_i = nc.dram_tensor("w2e", [DHID, C], F32, kind="ExternalInput").ap()
    b2_i = nc.dram_tensor("b2e", [1, C], F32, kind="ExternalInput").ap()
    meta = nc.dram_tensor("meta", [128, 4], F32, kind="ExternalInput").ap()

    out_t = nc.dram_tensor("out", [TOK, C], F32, kind="ExternalOutput").ap()

    wTd = {nm: {hl: nc.dram_tensor(f"w{nm}T{hl}", [C, C], BF16, kind="Internal").ap()
                for hl in ("h", "l")} for nm in ("q", "k", "v", "o")}
    xfh_d = nc.dram_tensor("xfh_d", [C, S], BF16, kind="Internal").ap()
    xfl_d = nc.dram_tensor("xfl_d", [C, S], BF16, kind="Internal").ap()
    ctab_d = nc.dram_tensor("ctab_d", [128, S], F32, kind="Internal").ap()
    stab_d = nc.dram_tensor("stab_d", [128, S], F32, kind="Internal").ap()
    v_d = nc.dram_tensor("v_d", [S, C], F32, kind="Internal").ap()
    ao_d = nc.dram_tensor("ao_d", [TOK, C], F32, kind="Internal").ap()
    x1_buf = nc.dram_tensor("x1_buf", [TOK, C], F32, kind="Internal").ap()
    mT_buf = nc.dram_tensor("mT_buf", [E, TOK], F32, kind="Internal").ap()
    x1_all = nc.dram_tensor("x1_all", [NTOT, C], F32, kind="Internal").ap()
    mT_all = nc.dram_tensor("mT_all", [NCORE * E, TOK], F32, kind="Internal").ap()
    cum_d = nc.dram_tensor("cum_d", [E, NTOT], F32, kind="Internal").ap()
    out2_b = nc.dram_tensor("out2_b", [CAP, C], F32, kind="Internal").ap()
    out2_all = nc.dram_tensor("out2_all", [E * CAP, C], F32, kind="Internal").ap()

    dbg_shapes = {"x1": [TOK, C], "gates": [128, NTILE * E], "mask": [E, TOK],
                  "ao": [TOK, C], "cum": [E, NTOT], "idx": [128, 20],
                  "out2": [CAP, C], "vdbg": [S, C],
                  "xed": [512, C], "hd": [DHID, 512], "ridx": [128, 2 * NTILE],
                  "tabs": [256, S], "colab": [128, 80]}
    dbg_t = {n: nc.dram_tensor("dbg_" + n, dbg_shapes[n], F32, kind="ExternalOutput").ap()
             for n in dbg}

    ident_i = nc.inline_tensor(np.eye(128, dtype=np.float32), name="ident").ap()
    colw_i = nc.inline_tensor(
        np.broadcast_to(2.0 ** -np.arange(E, dtype=np.float32), (128, E)).copy(), name="colw").ap()
    ecap_i = nc.inline_tensor(
        np.broadcast_to(np.arange(E, dtype=np.float32) * CAP, (128, E)).copy(), name="ecap").ap()
    iota_i = nc.inline_tensor(np.arange(128, dtype=np.float32).reshape(128, 1), name="iota128").ap()
    ones_r = nc.inline_tensor(np.ones((1, 128), dtype=np.float32), name="onesrow").ap()
    ones_c = nc.inline_tensor(np.ones((128, 1), dtype=np.float32), name="onescol").ap()

    sch = Sched(nc)

    with ExitStack() as _stk:
        def SB(name, shape, dt=F32):
            return _stk.enter_context(nc.sbuf_tensor(name, shape, dt))
        ids = SB("ids", [128, 128])
        meta_sb = SB("meta_sb", [128, 4])
        colsm = SB("colsm", [128, 80])
        gates_all = SB("gates_all", [128, NTILE * E])
        oh1_all = SB("oh1_all", [128, NTILE * E])
        oh2_all = SB("oh2_all", [128, NTILE * E])
        cw8 = SB("cw8", [128, E])
        ec8 = SB("ec8", [128, E])
        rbb = SB("rbb", [128, E])
        rwsl = SB("rwsl", [128, 2 * 64], BF16)
        idx_sb = SB("idx_sb", [128, 20], I32)
        i128 = SB("i128", [128, 1])
        onesr = SB("onesr", [128, 128])
        onesc = SB("onesc", [128, 1])
        pio2c = SB("pio2c", [128, 1])
        epsc = SB("epsc", [128, 1])
        PS = [_stk.enter_context(nc.psum_tensor(f"ps{i}", [128, 512], F32))
              for i in range(8)]
        ps7 = PS[7]
        block = _stk.enter_context(nc.Block())
        for sname in ("dS", "dG", "dA", "T", "V", "A", "G", "CC"):
            sch.sems[sname] = nc.alloc_semaphore(sname)

        EMIT, DMA = sch.emit, sch.dma

        def t_mm(ps_ap, lhsT, rhs, start, stop, waits=(), inc=None):
            def fn(e, o=ps_ap, a=lhsT, b=rhs, st=start, sp=stop):
                return e.matmul(o, a, b, start=st, stop=sp)
            return EMIT("tensor", fn, waits, inc=inc)

        def t_tr(ps_ap, in_ap, waits=(), inc=None):
            def fn(e, o=ps_ap, i=in_ap):
                return e.transpose(o, i, ids[:, :])
            return EMIT("tensor", fn, waits, inc=inc)

        def _outsz(args, kw):
            ap = kw.get("out")
            if ap is None and args:
                ap = args[0]
            try:
                return ap.free_size()
            except Exception:
                return 999999

        def v_op(name, *args, waits=(), inc=None, **kw):
            def fn(e, nm=name, a=args, k=kw):
                return getattr(e, nm)(*a, **k)
            return EMIT("vector", fn, waits, inc=inc,
                        post_drain=(_outsz(args, kw) < 256) or name == "tensor_tensor_scan")

        def a_op(name, *args, waits=(), inc=None, **kw):
            def fn(e, nm=name, a=args, k=kw):
                return getattr(e, nm)(*a, **k)
            return EMIT("scalar", fn, waits, inc=inc,
                        post_drain=_outsz(args, kw) < 256)

        def g_opx(name, *args, waits=(), inc=None, inc_by=1, **kw):
            def fn(e, nm=name, a=args, k=kw):
                return getattr(e, nm)(*a, **k)
            return EMIT("gpsimd", fn, waits, inc=inc, inc_by=inc_by)

        def mm3(ps_ap, ah, al, bh, bl, first, last, waits=(), inc=None):
            t_mm(ps_ap, ah, bh, first, False, waits=waits)
            t_mm(ps_ap, ah, bl, False, False)
            return t_mm(ps_ap, al, bh, False, last, inc=inc)

        def split(src, dsth, dstl, scratch, waits=()):
            e1 = v_op("tensor_copy", out=dsth, in_=src, waits=waits, inc="V")
            v_op("tensor_tensor", out=scratch, in0=src, in1=dsth, op=ALU.subtract)
            e2 = v_op("tensor_copy", out=dstl, in_=scratch, inc="V")
            return e1, e2

        def phase_sync():
            evs = []
            for sname in ("dS", "dG", "dA", "T", "V", "A", "G", "CC"):
                if sch.semval.get(sname):
                    evs.append((sname, sch.semval[sname], "__none__"))
            for eng in ENGS:
                for ev in evs:
                    sname, val, _ = ev
                    key = (eng, sname)
                    if sch.maxwait.get(key, -1) >= val:
                        continue
                EMIT(eng, lambda e: e.nop(), waits=tuple(evs))

        pid_cache = {}

        def core_reg(eng):
            if eng not in pid_cache:
                pid_cache[eng] = nc.partition_id(engines=[_E2T[eng]])
            return pid_cache[eng]

        e_ids = DMA("sync", ids[:, :], ident_i[:, :])
        DMA("sync", meta_sb[:, :], meta[:, :])
        DMA("sync", cw8[:, :], colw_i[:, :])
        e_ec8 = DMA("sync", ec8[:, :], ecap_i[:, :])
        e_i128 = DMA("sync", i128[:, :], iota_i[:, :])
        e_onesc = DMA("sync", onesc[:, :], ones_c[:, :])
        e_onesr = DMA("sync", onesr[0:1, :], ones_r[:, :])
        v_op("memset", pio2c[:, :], PIO2_HI)
        e_pio2 = v_op("memset", epsc[:, :], 1e-5, inc="V")

        with ExitStack() as _s0:
            def SB0(name, shape, dt=F32):
                return _s0.enter_context(nc.sbuf_tensor(name, shape, dt))
            stA = SB0("stA", [128, 1024])
            stB = SB0("stB", [128, 1024])
            w_hi = SB0("w_hi", [128, 1024], BF16)
            w_lo = SB0("w_lo", [128, 1024], BF16)
            tC = SB0("tC", [128, S])
            tS = SB0("tS", [128, S])
            tcol = SB0("tcol", [128, 16])
            ttmp = SB0("ttmp", [128, 64])
            pbT = SB0("pbT", [128, 16])
            e_rbl = DMA("sync", stA[0:1, 0:E], rb_i[:, :])
            ev = t_mm(ps7[:, 0:E], onesr[0:1, :], stA[0:1, 0:E], True, True,
                      waits=(e_rbl, e_onesr), inc="T")
            e_rbb = v_op("tensor_copy", out=rbb[:, :], in_=ps7[:, 0:E], waits=(ev,), inc="V")
            e_rwl = DMA("sync", stB[:, 0:64].rearrange("p (cc e) -> p cc e", cc=8),
                        rw_i.rearrange("(cc p) e -> p cc e", p=128), waits=(e_rbb,))
            _, e_rw = split(stB[:, 0:64], rwsl[:, 0:64], rwsl[:, 64:128],
                            stA[:, 0:64], waits=(e_rwl,))

            stA2 = SB0("stA2", [128, 1024])
            stB2 = SB0("stB2", [128, 1024])
            w_hi2 = SB0("w_hi2", [128, 1024], BF16)
            w_lo2 = SB0("w_lo2", [128, 1024], BF16)
            e_ws = {0: e_rw, 1: e_rw}
            tr_bank = {}
            ti = 0
            for nm, src in (("q", wq_i), ("k", wk_i), ("v", wv_i), ("o", wo_i)):
                for ot in range(8):
                    pp = ti % 2
                    ti += 1
                    sA, sB, wh, wl = ((stA, stB, w_hi, w_lo) if pp == 0
                                      else (stA2, stB2, w_hi2, w_lo2))
                    e_l = DMA("sync", sA[:, :], src[ot * 128:(ot + 1) * 128, :],
                              waits=(e_ws[pp],))
                    ev_t = None
                    for cc in range(8):
                        pst = PS[cc % 2][:, 0:128]
                        ev_t = t_tr(pst, sA[:, cc * 128:(cc + 1) * 128],
                                    waits=(e_l, e_ids, tr_bank.get(cc % 2)), inc="T")
                        tr_bank[cc % 2] = v_op("tensor_copy",
                                               out=sB[:, cc * 128:(cc + 1) * 128],
                                               in_=pst, waits=(ev_t,), inc="V")
                    e_h, e_lo = split(sB[:, :], wh[:, :], wl[:, :], sA[:, :],
                                      waits=(ev_t,))
                    DMA("sync",
                        wTd[nm]["h"].rearrange("(cc p) o -> p cc o", p=128)[:, :, ot * 128:(ot + 1) * 128],
                        wh.rearrange("p (cc o) -> p cc o", cc=8), waits=(e_h,))
                    e_ws[pp] = DMA("sync",
                                   wTd[nm]["l"].rearrange("(cc p) o -> p cc o", p=128)[:, :, ot * 128:(ot + 1) * 128],
                                   wl.rearrange("p (cc o) -> p cc o", cc=8), waits=(e_lo,))
            e_wT_done = e_ws[(ti - 1) % 2]

            v_op("tensor_scalar_mul", tcol[0:64, 0:1], i128[0:64, 0:1], 1.0 / 64.0,
                 waits=(e_i128,))
            v_op("tensor_scalar_mul", tcol[0:64, 1:2], tcol[0:64, 0:1], LN1E4_HI)
            v_op("tensor_scalar_mul", tcol[0:64, 2:3], tcol[0:64, 0:1], LN1E4_LO)
            ev = v_op("tensor_add", tcol[0:64, 1:2], tcol[0:64, 1:2], tcol[0:64, 2:3], inc="V")
            ev = a_op("activation", out=tcol[0:64, 0:1], in_=tcol[0:64, 1:2], func=AF.Exp,
                      scale=-1.0, waits=(ev,), inc="A")
            ev_s = a_op("activation", out=tS[0:64, 1:2], in_=tcol[0:64, 0:1], func=AF.Sin,
                        waits=(ev,), inc="A")
            ev2 = v_op("tensor_scalar_add", tcol[0:64, 3:4], tcol[0:64, 0:1], PIO2_LO,
                       waits=(ev,), inc="V")
            ev_c = a_op("activation", out=tC[0:64, 1:2], in_=tcol[0:64, 3:4], func=AF.Sin,
                        bias=pio2c[0:64, 0:1], waits=(ev2, e_pio2), inc="A")
            v_op("memset", tC[0:64, 0:1], 1.0)
            v_op("memset", tS[0:64, 0:1], 0.0)
            v_op("tensor_copy", out=tcol[0:64, 1:2], in_=tC[0:64, 1:2], waits=(ev_c,))
            v_op("tensor_copy", out=tcol[0:64, 2:3], in_=tS[0:64, 1:2], waits=(ev_s,))
            v_op("tensor_copy", out=tcol[0:64, 3:4], in_=tcol[0:64, 1:2])
            v_op("tensor_copy", out=tcol[0:64, 4:5], in_=tcol[0:64, 2:3])
            for rep in range(6):
                v_op("tensor_mul", tcol[0:64, 7:8], tcol[0:64, 3:4], tcol[0:64, 3:4])
                v_op("tensor_mul", tcol[0:64, 8:9], tcol[0:64, 4:5], tcol[0:64, 4:5])
                v_op("tensor_mul", tcol[0:64, 9:10], tcol[0:64, 3:4], tcol[0:64, 4:5])
                v_op("tensor_tensor", out=tcol[0:64, 3:4], in0=tcol[0:64, 7:8],
                     in1=tcol[0:64, 8:9], op=ALU.subtract)
                v_op("tensor_scalar_mul", tcol[0:64, 4:5], tcol[0:64, 9:10], 2.0)
                if rep == 2:
                    v_op("tensor_copy", out=tcol[0:64, 5:6], in_=tcol[0:64, 3:4])
                    v_op("tensor_copy", out=tcol[0:64, 6:7], in_=tcol[0:64, 4:5])
            def rot_step(dst_sl, src_sl, ccol, scol, tmp_sl):
                v_op("tensor_scalar", out=ttmp[0:64, tmp_sl], in0=tS[0:64, src_sl],
                     scalar1=scol, scalar2=None, op0=ALU.mult)
                v_op("scalar_tensor_tensor", out=tC[0:64, dst_sl], in0=tC[0:64, src_sl],
                     scalar=ccol, in1=ttmp[0:64, tmp_sl], op0=ALU.mult, op1=ALU.subtract)
                v_op("tensor_scalar", out=ttmp[0:64, tmp_sl], in0=tC[0:64, src_sl],
                     scalar1=scol, scalar2=None, op0=ALU.mult)
                v_op("scalar_tensor_tensor", out=tS[0:64, dst_sl], in0=tS[0:64, src_sl],
                     scalar=ccol, in1=ttmp[0:64, tmp_sl], op0=ALU.mult, op1=ALU.add)
            for s in range(2, 16):
                rot_step(slice(s, s + 1), slice(s - 1, s),
                         tcol[0:64, 1:2], tcol[0:64, 2:3], slice(0, 1))
            for blk in range(2, 16):
                s0 = blk * 8
                rot_step(slice(s0, s0 + 8), slice(s0 - 8, s0),
                         tcol[0:64, 5:6], tcol[0:64, 6:7], slice(0, 8))
            for blk in range(1, 64):
                s0 = blk * 64
                rot_step(slice(s0, s0 + 64), slice(s0 - 64, s0),
                         tcol[0:64, 3:4], tcol[0:64, 4:5], slice(0, 64))
            ev_tab = v_op("memset", tcol[0:64, 15:16], 0.0, inc="V")
            DMA("sync", ctab_d[0:64, :], tC[0:64, :], waits=(ev_tab,))
            DMA("sync", ctab_d[64:128, :], tC[0:64, :], waits=(ev_tab,))
            DMA("sync", stab_d[0:64, :], tS[0:64, :], waits=(ev_tab,))
            e_tabd = DMA("sync", stab_d[64:128, :], tS[0:64, :], waits=(ev_tab,))

            if "tabs" in dbg_t:
                DMA("gpsimd", dbg_t["tabs"][0:128, :], ctab_d[:, :], waits=(e_tabd,))
                DMA("gpsimd", dbg_t["tabs"][128:256, :], stab_d[:, :], waits=(e_tabd,))
            ev0 = v_op("memset", stA[:, 0:128], 0.0, inc="V")
            e_pl = DMA("sync", stB[0:16, 0:Dh], pope[:, :], waits=(ev0,))
            ev = v_op("tensor_scalar_max", stA[0:16, 0:Dh], stB[0:16, 0:Dh], -TWOPI,
                      waits=(e_pl,))
            ev = v_op("tensor_scalar_min", stA[0:16, 0:Dh], stA[0:16, 0:Dh], 0.0)
            ev = v_op("tensor_scalar_mul", stA[0:16, 0:Dh], stA[0:16, 0:Dh], 0.5, inc="V")
            ev_sh = a_op("activation", out=stB[0:16, 0:Dh], in_=stA[0:16, 0:Dh],
                         func=AF.Sin, waits=(ev,), inc="A")
            ev2 = v_op("tensor_scalar_add", stA[0:16, 64:128], stA[0:16, 0:Dh], PIO2_LO,
                       waits=(ev,), inc="V")
            ev_ch = a_op("activation", out=stB[0:16, 64:128], in_=stA[0:16, 64:128],
                         func=AF.Sin, bias=pio2c[0:16, 0:1], waits=(ev2, e_pio2), inc="A")
            v_op("tensor_mul", stA[0:16, 0:Dh], stB[0:16, 0:Dh], stB[0:16, 0:Dh],
                 waits=(ev_sh,))
            v_op("tensor_scalar", out=stA[0:16, 0:Dh], in0=stA[0:16, 0:Dh], scalar1=-2.0,
                 scalar2=1.0, op0=ALU.mult, op1=ALU.add)
            v_op("tensor_mul", stA[0:16, 64:128], stB[0:16, 0:Dh], stB[0:16, 64:128],
                 waits=(ev_ch,))
            ev = v_op("tensor_scalar_mul", stA[0:16, 64:128], stA[0:16, 64:128], 2.0,
                      inc="V")
            evt = t_tr(ps7[:, 0:128], stA[:, 0:128], waits=(ev, e_ids), inc="T")
            ev = v_op("tensor_copy", out=pbT[:, 0:16], in_=ps7[:, 0:16], waits=(evt,),
                      inc="V")
            v_op("tensor_copy", out=colsm[0:64, 0:16], in_=pbT[0:64, 0:16])
            ev_a = v_op("tensor_scalar_mul", colsm[64:128, 0:16], pbT[64:128, 0:16],
                        -1.0, inc="V")
            DMA("gpsimd", colsm[0:64, 16:32], pbT[64:128, 0:16], waits=(ev,))
            e_AB = DMA("gpsimd", colsm[64:128, 16:32], pbT[0:64, 0:16], waits=(ev,))

            if "colab" in dbg_t:
                DMA("gpsimd", dbg_t["colab"][:, :], colsm[:, :], waits=(e_AB, ev_a))
            e_xf = e_AB
            for tt in range(32):
                e_l = DMA("sync", stA[:, :], xb[tt * 128:(tt + 1) * 128, :], waits=(e_xf,))
                ev_t = None
                for cc in range(8):
                    pst = PS[cc % 2][:, 0:128]
                    ev_t = t_tr(pst, stA[:, cc * 128:(cc + 1) * 128],
                                waits=(e_l, tr_bank.get(cc % 2)), inc="T")
                    tr_bank[cc % 2] = v_op("tensor_copy",
                                           out=stB[:, cc * 128:(cc + 1) * 128],
                                           in_=pst, waits=(ev_t,), inc="V")
                e_h, e_lo = split(stB[:, :], w_hi[:, :], w_lo[:, :], stA[:, :],
                                  waits=(ev_t,))
                DMA("sync", xfh_d.rearrange("(cc p) t -> p cc t", p=128)[:, :, tt * 128:(tt + 1) * 128],
                    w_hi.rearrange("p (cc t) -> p cc t", cc=8), waits=(e_h,))
                e_xf = DMA("sync", xfl_d.rearrange("(cc p) t -> p cc t", p=128)[:, :, tt * 128:(tt + 1) * 128],
                           w_lo.rearrange("p (cc t) -> p cc t", cc=8), waits=(e_lo,))
            e_xf_done = e_xf

        if phases >= 2:
            phase_sync()
            with ExitStack() as _s2:
                def SB2(name, shape, dt=F32):
                    return _s2.enter_context(nc.sbuf_tensor(name, shape, dt))
                xs_h = SB2("xs_h", [128, 8 * 512], BF16)
                xs_l = SB2("xs_l", [128, 8 * 512], BF16)
                wv_h = SB2("wv_h", [128, 8 * 1024], BF16)
                wv_l = SB2("wv_l", [128, 8 * 1024], BF16)
                vst = SB2("vst", [128, 1024])
                DMA("sync", wv_h.rearrange("p (cc o) -> p cc o", cc=8),
                    wTd["v"]["h"].rearrange("(cc p) o -> p cc o", p=128), waits=(e_wT_done,))
                e_wv = DMA("sync", wv_l.rearrange("p (cc o) -> p cc o", cc=8),
                           wTd["v"]["l"].rearrange("(cc p) o -> p cc o", p=128),
                           waits=(e_wT_done,))
                e_vs = None
                bankev = {}
                for tc in range(8):
                    e_x1 = DMA("sync", xs_h.rearrange("p (cc t) -> p cc t", cc=8),
                               xfh_d.rearrange("(cc p) t -> p cc t", p=128)[:, :, tc * 512:(tc + 1) * 512],
                               waits=(e_xf_done,))
                    e_x2 = DMA("sync", xs_l.rearrange("p (cc t) -> p cc t", cc=8),
                               xfl_d.rearrange("(cc p) t -> p cc t", p=128)[:, :, tc * 512:(tc + 1) * 512],
                               waits=(e_xf_done,))
                    for tt in range(4):
                        ev_e = None
                        for oh in range(2):
                            bk = 2 * (tt % 2) + oh
                            pst = PS[bk]
                            ev = None
                            for cc in range(8):
                                ah = xs_h[:, cc * 512 + tt * 128: cc * 512 + (tt + 1) * 128]
                                al = xs_l[:, cc * 512 + tt * 128: cc * 512 + (tt + 1) * 128]
                                bh = wv_h[:, cc * 1024 + oh * 512: cc * 1024 + (oh + 1) * 512]
                                bl = wv_l[:, cc * 1024 + oh * 512: cc * 1024 + (oh + 1) * 512]
                                ev = mm3(pst[:, :], ah, al, bh, bl, cc == 0, cc == 7,
                                         waits=(e_x1, e_x2, e_wv, bankev.get(bk)),
                                         inc="T" if cc == 7 else None)
                            ev_e = v_op("tensor_copy", out=vst[:, oh * 512:(oh + 1) * 512],
                                        in_=pst[:, :], waits=(ev, e_vs), inc="V")
                            bankev[bk] = ev_e
                        e_vs = DMA("sync", v_d[tc * 512 + tt * 128: tc * 512 + (tt + 1) * 128, :],
                                   vst[:, :], waits=(ev_e,))
                e_v_done = e_vs
                if "vdbg" in dbg_t:
                    e_v_done = DMA("gpsimd", dbg_t["vdbg"][:, :], v_d[:, :], waits=(e_v_done,))

        if phases >= 3:
            phase_sync()
            with ExitStack() as _s3:
                def SB3(name, shape, dt=F32):
                    return _s3.enter_context(nc.sbuf_tensor(name, shape, dt))
                tqCS = SB3("tqCS", [128, 2 * 512])
                tkCS = SB3("tkCS", [128, 2 * 2048])
                xs_h2 = SB3("xs_h2", [128, 8 * 512], BF16)
                xs_l2 = SB3("xs_l2", [128, 8 * 512], BF16)
                wsl = SB3("wsl", [128, 4 * 1024], BF16)
                magq = SB3("magq", [128, 2048])
                mgqx = SB3("mgqx", [128, 2048])
                qstk = SB3("qstk", [128, 2048])
                qhi = SB3("qhi", [128, 2 * 2048], BF16)
                qlo = SB3("qlo", [128, 2 * 2048], BF16)
                magk = SB3("magk", [128, 2048])
                mgkx = SB3("mgkx", [128, 2048])
                khi = SB3("khi", [128, 2 * 4096], BF16)
                klo = SB3("klo", [128, 2 * 4096], BF16)
                vhd = SB3("vhd", [128, 32 * 66])
                at0 = SB3("at0", [128, 512])
                at1 = SB3("at1", [128, 512])
                at2 = SB3("at2", [128, 512])
                at3 = SB3("at3", [128, 512])
                aoev = SB3("aoev", [128, 256])
                spst = SB3("spst", [128, 512])
                eqA = SB3("eqA", [128, 512])
                eqB = SB3("eqB", [128, 512])
                ekst = SB3("ekst", [128, 2048])
                spl = SB3("spl", [128, 2048])
                sqst = SB3("sqst", [128, 2048])
                ATT = [at0, at1, at2, at3]
                qb_g = core_reg("gpsimd")
                qbase_g = (qb_g % 2) * TOK

                ev_avE = {}
                ev_expA = {}
                ev_atfree = {}
                ev_sq_free = None
                e_tq_free = None
                e_tk_free = None
                e_xs2_free = None
                e_ao = None
                ev_ps6 = None
                ev_ps7 = None
                e_wsl_free = None

                for p in range(8):
                    h0 = 2 * p
                    for j, (nm, hl) in enumerate((("q", "h"), ("q", "l"), ("k", "h"), ("k", "l"))):
                        e_wsl = DMA("sync",
                                    wsl.rearrange("p (j cc o) -> p j cc o", j=4, cc=8)[:, j, :, :],
                                    wTd[nm][hl].rearrange("(cc p) o -> p cc o", p=128)[:, :, p * 128:(p + 1) * 128],
                                    waits=(e_wT_done, e_wsl_free))
                    Wq_h = wsl[:, 0:1024]
                    Wq_l = wsl[:, 1024:2048]
                    Wk_h = wsl[:, 2048:3072]
                    Wk_l = wsl[:, 3072:4096]

                    ev_mq = None
                    for qc in range(4):
                        e_x1 = DMA("gpsimd", xs_h2.rearrange("p (cc t) -> p cc t", cc=8),
                                   xfh_d.rearrange("(cc p) t -> p cc t", p=128)
                                   [:, :, bass.ds(qbase_g + qc * 512, 512)],
                                   waits=(e_xf_done, e_xs2_free))
                        e_x2 = DMA("gpsimd", xs_l2.rearrange("p (cc t) -> p cc t", cc=8),
                                   xfl_d.rearrange("(cc p) t -> p cc t", p=128)
                                   [:, :, bass.ds(qbase_g + qc * 512, 512)],
                                   waits=(e_xf_done,))
                        ev = None
                        for cc in range(8):
                            ev = mm3(PS[6][:, :],
                                     Wq_h[:, cc * 128:(cc + 1) * 128],
                                     Wq_l[:, cc * 128:(cc + 1) * 128],
                                     xs_h2[:, cc * 512:(cc + 1) * 512],
                                     xs_l2[:, cc * 512:(cc + 1) * 512],
                                     cc == 0, cc == 7,
                                     waits=(e_x1, e_x2, e_wsl, ev_ps6),
                                     inc="T" if cc == 7 else None)
                        e_xs2_free = ev
                        ev_ps6 = a_op("activation", out=spst[:, :], in_=PS[6][:, :],
                                      func=AF.Exp, waits=(ev,), inc="A")
                        ev_mq = a_op("activation", out=magq[:, qc * 512:(qc + 1) * 512],
                                     in_=spst[:, :], func=AF.Ln, bias=onesc[:, 0:1], inc="A")
                    DMA("gpsimd", mgqx[64:128, :], magq[0:64, :], waits=(ev_mq,))
                    e_gq = DMA("gpsimd", mgqx[0:64, :], magq[64:128, :], waits=(ev_mq,))

                    e_qsplit = {}
                    ev_qb2 = {}
                    for hh in range(2):
                        hname = h0 + hh
                        Acol = colsm[:, hname:hname + 1]
                        Bcol = colsm[:, 16 + hname:16 + hname + 1]
                        mt = magq if hh == 0 else mgqx
                        mb = mgqx if hh == 0 else magq
                        for qc in range(4):
                            e_t1 = DMA("gpsimd", tqCS[:, 0:512],
                                       ctab_d[:, bass.ds(qbase_g + qc * 512, 512)],
                                       waits=(e_tabd, e_tq_free))
                            e_t2 = DMA("gpsimd", tqCS[:, 512:1024],
                                       stab_d[:, bass.ds(qbase_g + qc * 512, 512)],
                                       waits=(e_tabd,))
                            sl = slice(qc * 512, (qc + 1) * 512)
                            v_op("tensor_tensor", out=eqA[0:64, :], in0=mt[0:64, sl],
                                 in1=tqCS[0:64, 0:512], op=ALU.mult,
                                 waits=(e_t1, e_gq, e_AB))
                            v_op("tensor_tensor", out=eqA[64:128, :], in0=mb[64:128, sl],
                                 in1=tqCS[64:128, 0:512], op=ALU.mult)
                            v_op("tensor_tensor", out=eqB[0:64, :], in0=mt[0:64, sl],
                                 in1=tqCS[0:64, 512:1024], op=ALU.mult, waits=(e_t2,))
                            e_tq_free = v_op("tensor_tensor", out=eqB[64:128, :],
                                             in0=mb[64:128, sl],
                                             in1=tqCS[64:128, 512:1024], op=ALU.mult,
                                             inc="V")
                            v_op("tensor_scalar", out=eqB[:, :], in0=eqB[:, :],
                                 scalar1=Bcol, scalar2=None, op0=ALU.mult)
                            ev_qbld = v_op("scalar_tensor_tensor", out=qstk[:, sl],
                                           in0=eqA[:, :], scalar=Acol, in1=eqB[:, :],
                                           op0=ALU.mult, op1=ALU.add, inc="V")
                        qsl = slice(hh * 2048, (hh + 1) * 2048)
                        _, e_ql = split(qstk[:, :], qhi[:, qsl], qlo[:, qsl], spl[:, :])
                        e_qsplit[hh] = e_ql
                        ev_sq = a_op("activation", out=sqst[:, :], in_=qstk[:, :],
                                     func=AF.Square, waits=(ev_sq_free, ev_qbld), inc="A")
                        ev_r = v_op("memset", colsm[0:1, 48:49], 0.0, waits=(ev_ps7,), inc="V")
                        for ch in range(4):
                            evm = t_mm(ps7[0:1, 0:512], onesc[:, :],
                                       sqst[:, ch * 512:(ch + 1) * 512], True, True,
                                       waits=(ev_sq, e_onesc, ev_r), inc="T")
                            v_op("tensor_reduce", out=colsm[0:1, 64:65], in_=ps7[0:1, 0:512],
                                 axis=mybir.AxisListType.X, op=ALU.max, waits=(evm,))
                            ev_r = v_op("tensor_tensor", out=colsm[0:1, 48:49],
                                        in0=colsm[0:1, 48:49], in1=colsm[0:1, 64:65],
                                        op=ALU.max, inc="V")
                            ev_sq_free = evm
                            ev_ps7 = ev_r
                        ev_qb2[hh] = ev_r
                        v_op("tensor_copy", out=colsm[0:1, 50 + hh:51 + hh],
                             in_=colsm[0:1, 48:49])

                    e_ksplit = {}
                    ev_mk2 = {0: None, 1: None}
                    ev_r = v_op("memset", colsm[0:1, 66:68], 0.0, waits=(ev_ps7,), inc="V")
                    ev_ps7 = ev_r
                    for half in range(2):
                        tb = half * 2048
                        ev_mk = None
                        for tc4 in range(4):
                            tg = tb + tc4 * 512
                            e_x1 = DMA("sync", xs_h2.rearrange("p (cc t) -> p cc t", cc=8),
                                       xfh_d.rearrange("(cc p) t -> p cc t", p=128)[:, :, tg:tg + 512],
                                       waits=(e_xf_done, e_xs2_free))
                            e_x2 = DMA("sync", xs_l2.rearrange("p (cc t) -> p cc t", cc=8),
                                       xfl_d.rearrange("(cc p) t -> p cc t", p=128)[:, :, tg:tg + 512],
                                       waits=(e_xf_done,))
                            ev = None
                            for cc in range(8):
                                ev = mm3(PS[6][:, :],
                                         Wk_h[:, cc * 128:(cc + 1) * 128],
                                         Wk_l[:, cc * 128:(cc + 1) * 128],
                                         xs_h2[:, cc * 512:(cc + 1) * 512],
                                         xs_l2[:, cc * 512:(cc + 1) * 512],
                                         cc == 0, cc == 7,
                                         waits=(e_x1, e_x2, e_wsl, ev_ps6),
                                         inc="T" if cc == 7 else None)
                            e_xs2_free = ev
                            ev_ps6 = a_op("activation", out=spst[:, :], in_=PS[6][:, :],
                                          func=AF.Exp, waits=(ev,), inc="A")
                            ev_mk = a_op("activation", out=magk[:, tc4 * 512:(tc4 + 1) * 512],
                                         in_=spst[:, :], func=AF.Ln, bias=onesc[:, 0:1], inc="A")
                        DMA("gpsimd", mgkx[64:128, :], magk[0:64, :], waits=(ev_mk,))
                        e_gk = DMA("gpsimd", mgkx[0:64, :], magk[64:128, :], waits=(ev_mk,))
                        e_tk1 = DMA("sync", tkCS[:, 0:2048], ctab_d[:, tb:tb + 2048],
                                    waits=(e_tabd, e_tk_free))
                        e_tk2 = DMA("sync", tkCS[:, 2048:4096], stab_d[:, tb:tb + 2048],
                                    waits=(e_tabd,))
                        for hh in range(2):
                            mt = magk if hh == 0 else mgkx
                            mb = mgkx if hh == 0 else magk
                            v_op("tensor_tensor", out=ekst[0:64, :], in0=mt[0:64, :],
                                 in1=tkCS[0:64, 0:2048], op=ALU.mult, waits=(e_tk1, e_gk))
                            ev_k = v_op("tensor_tensor", out=ekst[64:128, :],
                                        in0=mb[64:128, :],
                                        in1=tkCS[64:128, 2048:4096], op=ALU.mult,
                                        waits=(e_tk2,), inc="V")
                            e_tk_free = ev_k
                            ksl = slice(hh * 4096 + tb, hh * 4096 + tb + 2048)
                            _, e_kl = split(ekst[:, :], khi[:, ksl], klo[:, ksl], spl[:, :])
                            e_ksplit[hh] = e_kl
                            ev_ksq = a_op("activation", out=sqst[:, :], in_=ekst[:, :],
                                          func=AF.Square, waits=(ev_sq_free, ev_k), inc="A")
                            for ch in range(4):
                                evm = t_mm(ps7[0:1, 0:512], onesc[:, :],
                                           sqst[:, ch * 512:(ch + 1) * 512], True, True,
                                           waits=(ev_ksq, ev_ps7), inc="T")
                                v_op("tensor_reduce", out=colsm[0:1, 64:65],
                                     in_=ps7[0:1, 0:512], axis=mybir.AxisListType.X,
                                     op=ALU.max, waits=(evm,))
                                ev_r = v_op("tensor_tensor", out=colsm[0:1, 66 + hh:67 + hh],
                                            in0=colsm[0:1, 66 + hh:67 + hh],
                                            in1=colsm[0:1, 64:65], op=ALU.max, inc="V")
                                ev_sq_free = evm
                                ev_ps7 = ev_r
                                ev_mk2[hh] = ev_r

                    ev_negc = {}
                    for hh in range(2):
                        hname = h0 + hh
                        ev = v_op("tensor_tensor", out=colsm[0:1, 64:65],
                                  in0=colsm[0:1, 50 + hh:51 + hh],
                                  in1=colsm[0:1, 66 + hh:67 + hh], op=ALU.mult,
                                  waits=(ev_mk2[hh], ev_qb2[hh]), inc="V")
                        ev = a_op("activation", out=colsm[0:1, 64:65], in_=colsm[0:1, 64:65],
                                  func=AF.Ln, waits=(ev,), inc="A")
                        ev = a_op("activation", out=colsm[0:1, 64:65], in_=colsm[0:1, 64:65],
                                  func=AF.Exp, scale=0.5, inc="A")
                        ev = v_op("tensor_scalar_mul", colsm[0:1, 65:66], colsm[0:1, 64:65],
                                  -0.125, waits=(ev,), inc="V")
                        evb = t_mm(ps7[:, 0:1], onesr[0:1, :], colsm[0:1, 65:66], True, True,
                                   waits=(ev, ev_ps7), inc="T")
                        ev_negc[hh] = v_op("tensor_copy",
                                           out=colsm[:, 32 + hname:33 + hname],
                                           in_=ps7[:, 0:1], waits=(evb,), inc="V")
                        ev_ps7 = ev_negc[hh]

                    for hh in range(2):
                        hname = h0 + hh
                        e_vl = DMA("sync",
                                   vhd.rearrange("p (tt o) -> p tt o", tt=32)[:, :, 0:64],
                                   v_d.rearrange("(tt p) o -> p tt o", p=128)
                                   [:, :, hname * 64:(hname + 1) * 64],
                                   waits=(e_v_done, ev_avE.get(3)))
                        ev_ones = v_op("memset",
                                       vhd.rearrange("p (tt o) -> p tt o", tt=32)[:, :, 64:65],
                                       1.0, inc="V")
                        negC = colsm[:, 32 + hname:33 + hname]
                        for qc in range(4):
                            ev_av_stop = {}
                            for tt in range(32):
                                bk = 4 + (tt % 2)
                                psS = PS[bk]
                                ksl0 = hh * 4096 + tt * 128
                                qsl = slice(hh * 2048 + qc * 512, hh * 2048 + (qc + 1) * 512)
                                ev_s3 = mm3(psS[:, :],
                                            khi[:, ksl0:ksl0 + 128], klo[:, ksl0:ksl0 + 128],
                                            qhi[:, qsl], qlo[:, qsl],
                                            True, True,
                                            waits=(e_ksplit[hh], e_qsplit[hh],
                                                   ev_expA.get(bk)),
                                            inc="T")
                                at = ATT[tt % 4]
                                ev_exp = a_op("activation", out=at[:, :], in_=psS[:, :],
                                              func=AF.Exp, bias=negC, scale=0.125,
                                              waits=(ev_s3, ev_negc[hh],
                                                     ev_atfree.get(tt % 4)),
                                              inc="A")
                                ev_expA[bk] = ev_exp
                                for qt in range(4):
                                    want_inc = (qt == 3) or (tt == 31)
                                    evv = t_mm(PS[qt][:, 0:65],
                                               at[:, qt * 128:(qt + 1) * 128],
                                               vhd[:, tt * 66:tt * 66 + 65],
                                               tt == 0, tt == 31,
                                               waits=(ev_exp, e_vl, ev_ones,
                                                      ev_avE.get(qt) if tt == 0 else None),
                                               inc="T" if want_inc else None)
                                    if tt == 31:
                                        ev_av_stop[qt] = evv
                                ev_atfree[tt % 4] = evv
                            e_ao_prev = e_ao
                            for qt in range(4):
                                v_op("reciprocal", out=colsm[:, 64:65],
                                     in_=PS[qt][:, 64:65],
                                     waits=(ev_av_stop[qt], e_ao_prev))
                                ev_avE[qt] = v_op("tensor_scalar",
                                                  out=aoev[:, qt * 64:(qt + 1) * 64],
                                                  in0=PS[qt][:, 0:64],
                                                  scalar1=colsm[:, 64:65],
                                                  scalar2=None, op0=ALU.mult, inc="V")
                            e_ao = DMA("sync",
                                       ao_d.rearrange("(qt p) o -> p qt o", p=128)
                                       [:, 4 * qc:4 * qc + 4, hname * 64:(hname + 1) * 64],
                                       aoev.rearrange("p (qt o) -> p qt o", qt=4),
                                       waits=(ev_avE[0], ev_avE[1], ev_avE[2], ev_avE[3]))
                    e_wsl_free = e_xs2_free
                e_att_done = e_ao
                if "ao" in dbg_t:
                    e_att_done = DMA("gpsimd", dbg_t["ao"][:, :], ao_d[:, :],
                                     waits=(e_att_done,))

        if phases >= 4:
            phase_sync()
            with ExitStack() as _s4:
                def SB4(name, shape, dt=F32):
                    return _s4.enter_context(nc.sbuf_tensor(name, shape, dt))
                wosl = SB4("wosl", [128, 2 * 8 * 1024], BF16)
                aot = SB4("aot", [128, 1024])
                xot = SB4("xot", [128, 1024])
                aofh = SB4("aofh", [128, 1024], BF16)
                aofl = SB4("aofl", [128, 1024], BF16)
                x1t = SB4("x1t", [128, 1024])
                x1fh = SB4("x1fh", [128, 1024], BF16)
                x1fl = SB4("x1fl", [128, 1024], BF16)
                p4a = SB4("p4a", [128, 1024])
                p4b = SB4("p4b", [128, 1024])
                mk8 = SB4("mk8", [128, 128])

                DMA("sync", wosl[:, 0:8192].rearrange("p (cc o) -> p cc o", cc=8),
                    wTd["o"]["h"].rearrange("(cc p) o -> p cc o", p=128), waits=(e_wT_done,))
                e_wo = DMA("sync", wosl[:, 8192:16384].rearrange("p (cc o) -> p cc o", cc=8),
                           wTd["o"]["l"].rearrange("(cc p) o -> p cc o", p=128),
                           waits=(e_wT_done,))
                ev_mz = v_op("memset", mk8[:, :], 0.0, inc="V")
                e_x1s = None
                e_mts = None
                ev_p4 = {}
                ev_tfree = None
                for tl in range(NTILE):
                    e_aol = DMA("sync", aot[:, :], ao_d[tl * 128:(tl + 1) * 128, :],
                                waits=(e_att_done, ev_tfree))
                    e_xol = DMA("sync", xot[:, :],
                                xq[tl * 128:(tl + 1) * 128, :],
                                waits=(ev_tfree,))
                    ev_t = None
                    for cc in range(8):
                        pst = PS[6][:, 0:128]
                        ev_t = t_tr(pst, aot[:, cc * 128:(cc + 1) * 128],
                                    waits=(e_aol, ev_p4.get(6)), inc="T")
                        ev_p4[6] = v_op("tensor_copy", out=p4a[:, cc * 128:(cc + 1) * 128],
                                        in_=pst, waits=(ev_t,), inc="V")
                    e_aoh, e_aolo = split(p4a[:, :], aofh[:, :], aofl[:, :], p4b[:, :],
                                          waits=(ev_t,))
                    for ohh in range(2):
                        bk = ohh
                        ev = None
                        for cc in range(8):
                            ev = mm3(PS[bk][:, :],
                                     aofh[:, cc * 128:(cc + 1) * 128],
                                     aofl[:, cc * 128:(cc + 1) * 128],
                                     wosl[:, cc * 1024 + ohh * 512: cc * 1024 + (ohh + 1) * 512],
                                     wosl[:, 8192 + cc * 1024 + ohh * 512: 8192 + cc * 1024 + (ohh + 1) * 512],
                                     cc == 0, cc == 7,
                                     waits=(e_aoh, e_aolo, e_wo, ev_p4.get(bk)),
                                     inc="T" if cc == 7 else None)
                        ev_p4[bk] = v_op("tensor_tensor",
                                         out=x1t[:, ohh * 512:(ohh + 1) * 512],
                                         in0=PS[bk][:, :],
                                         in1=xot[:, ohh * 512:(ohh + 1) * 512],
                                         op=ALU.add, waits=(ev, e_xol), inc="V")
                    ev_res = ev_p4[1]
                    ev_sum = v_op("tensor_reduce", out=colsm[:, 64:65], in_=x1t[:, :],
                                  axis=mybir.AxisListType.X, op=ALU.add, waits=(ev_res,),
                                  inc="V")
                    ev_ssq = a_op("activation", out=p4b[:, 0:1024], in_=x1t[:, :],
                                  func=AF.Square, accum_out=colsm[:, 65:66],
                                  waits=(ev_res,), inc="A")
                    v_op("tensor_scalar_mul", colsm[:, 64:65], colsm[:, 64:65], 1.0 / C,
                         waits=(ev_sum,))
                    v_op("tensor_scalar_mul", colsm[:, 66:67], colsm[:, 65:66], 1.0 / C,
                         waits=(ev_ssq,))
                    v_op("tensor_mul", colsm[:, 67:68], colsm[:, 64:65], colsm[:, 64:65])
                    ev = v_op("tensor_tensor", out=colsm[:, 66:67], in0=colsm[:, 66:67],
                              in1=colsm[:, 67:68], op=ALU.subtract, inc="V")
                    ev = a_op("activation", out=colsm[:, 66:67], in_=colsm[:, 66:67],
                              func=AF.Ln, bias=epsc[:, 0:1], waits=(ev,), inc="A")
                    ev = a_op("activation", out=colsm[:, 66:67], in_=colsm[:, 66:67],
                              func=AF.Exp, scale=-0.5, inc="A")
                    ev = v_op("tensor_mul", colsm[:, 67:68], colsm[:, 64:65], colsm[:, 66:67],
                              waits=(ev,))
                    ev_nmr = v_op("tensor_scalar_mul", colsm[:, 67:68], colsm[:, 67:68], -1.0,
                                  inc="V")
                    ev_x1 = v_op("tensor_scalar", out=x1t[:, :], in0=x1t[:, :],
                                 scalar1=colsm[:, 66:67], scalar2=colsm[:, 67:68],
                                 op0=ALU.mult, op1=ALU.add, waits=(ev_nmr,), inc="V")
                    e_x1s = DMA("sync", x1_buf[tl * 128:(tl + 1) * 128, :], x1t[:, :],
                                waits=(ev_x1,))
                    ev_t2 = None
                    for cc in range(8):
                        pst = PS[6][:, 0:128]
                        ev_t2 = t_tr(pst, x1t[:, cc * 128:(cc + 1) * 128],
                                     waits=(ev_x1, ev_p4.get(6)), inc="T")
                        ev_p4[6] = v_op("tensor_copy", out=p4a[:, cc * 128:(cc + 1) * 128],
                                        in_=pst, waits=(ev_t2,), inc="V")
                    e_xh, e_xl = split(p4a[:, :], x1fh[:, :], x1fl[:, :], p4b[:, :],
                                       waits=(ev_t2,))
                    ev = None
                    for cc in range(8):
                        ev = mm3(PS[7][:, 0:E],
                                 x1fh[:, cc * 128:(cc + 1) * 128],
                                 x1fl[:, cc * 128:(cc + 1) * 128],
                                 rwsl[:, cc * 8:(cc + 1) * 8],
                                 rwsl[:, 64 + cc * 8:64 + (cc + 1) * 8],
                                 cc == 0, cc == 7,
                                 waits=(e_xh, e_xl, e_rw, ev_p4.get(7)),
                                 inc="T" if cc == 7 else None)
                    gsl = slice(tl * E, (tl + 1) * E)
                    ev_lg = v_op("tensor_tensor", out=gates_all[:, gsl], in0=PS[7][:, 0:E],
                                 in1=rbb[:, :], op=ALU.add, waits=(ev, e_rbb), inc="V")
                    ev_p4[7] = ev_lg
                    v_op("tensor_reduce", out=colsm[:, 64:65], in_=gates_all[:, gsl],
                         axis=mybir.AxisListType.X, op=ALU.max)
                    ev = v_op("tensor_scalar_mul", colsm[:, 64:65], colsm[:, 64:65], -1.0,
                              inc="V")
                    ev = a_op("activation", out=gates_all[:, gsl], in_=gates_all[:, gsl],
                              func=AF.Exp, bias=colsm[:, 64:65],
                              accum_out=colsm[:, 65:66], waits=(ev,), inc="A")
                    v_op("reciprocal", out=colsm[:, 65:66], in_=colsm[:, 65:66], waits=(ev,))
                    ev_g = v_op("tensor_scalar", out=gates_all[:, gsl],
                                in0=gates_all[:, gsl], scalar1=colsm[:, 65:66],
                                scalar2=None, op0=ALU.mult, inc="V")
                    v_op("tensor_reduce", out=colsm[:, 64:65], in_=gates_all[:, gsl],
                         axis=mybir.AxisListType.X, op=ALU.max)
                    v_op("tensor_scalar", out=p4a[:, 0:E], in0=gates_all[:, gsl],
                         scalar1=colsm[:, 64:65], scalar2=None, op0=ALU.is_equal)
                    v_op("tensor_tensor", out=p4a[:, 0:E], in0=p4a[:, 0:E], in1=cw8[:, :],
                         op=ALU.mult)
                    v_op("tensor_reduce", out=colsm[:, 64:65], in_=p4a[:, 0:E],
                         axis=mybir.AxisListType.X, op=ALU.max)
                    v_op("tensor_scalar", out=oh1_all[:, gsl], in0=p4a[:, 0:E],
                         scalar1=colsm[:, 64:65], scalar2=None, op0=ALU.is_equal)
                    v_op("scalar_tensor_tensor", out=p4a[:, 0:E], in0=oh1_all[:, gsl],
                         scalar=-2.0, in1=gates_all[:, gsl], op0=ALU.mult, op1=ALU.add)
                    v_op("tensor_reduce", out=colsm[:, 64:65], in_=p4a[:, 0:E],
                         axis=mybir.AxisListType.X, op=ALU.max)
                    v_op("tensor_scalar", out=p4b[:, 0:E], in0=p4a[:, 0:E],
                         scalar1=colsm[:, 64:65], scalar2=None, op0=ALU.is_equal)
                    v_op("tensor_tensor", out=p4b[:, 0:E], in0=p4b[:, 0:E], in1=cw8[:, :],
                         op=ALU.mult)
                    v_op("tensor_reduce", out=colsm[:, 64:65], in_=p4b[:, 0:E],
                         axis=mybir.AxisListType.X, op=ALU.max)
                    v_op("tensor_scalar", out=oh2_all[:, gsl], in0=p4b[:, 0:E],
                         scalar1=colsm[:, 64:65], scalar2=None, op0=ALU.is_equal)
                    ev_m = v_op("tensor_tensor", out=mk8[:, 0:E], in0=oh1_all[:, gsl],
                                in1=oh2_all[:, gsl], op=ALU.add,
                                waits=(ev_mz, ev_p4.get("mktr")), inc="V")
                    evt = t_tr(PS[6][:, 0:128], mk8[:, :], waits=(ev_m, ev_p4.get(6)),
                               inc="T")
                    ev_p4["mktr"] = evt
                    ev_p4[6] = v_op("tensor_copy", out=p4a[0:E, 0:128], in_=PS[6][0:E, 0:128],
                                    waits=(evt,), inc="V")
                    ev_tfree = ev_p4[6]
                    e_mts = DMA("sync", mT_buf[:, tl * 128:(tl + 1) * 128], p4a[0:E, 0:128],
                                waits=(ev_p4[6],))
                e_x1_done, e_mt_done = e_x1s, e_mts
                if "x1" in dbg_t:
                    e_x1_done = DMA("gpsimd", dbg_t["x1"][:, :], x1_buf[:, :],
                                    waits=(e_x1_done,))
                if "mask" in dbg_t:
                    e_mt_done = DMA("gpsimd", dbg_t["mask"][:, :], mT_buf[:, :],
                                    waits=(e_mt_done,))
                if "gates" in dbg_t:
                    EMIT("gpsimd", lambda e: e.dma_start(out=dbg_t["gates"][:, :],
                                                         in_=gates_all[:, :]),
                         waits=(ev_g,), inc="dG", inc_by=16)

        if phases >= 5:
            phase_sync()
            with ExitStack() as _s5:
                def SB5(name, shape, dt=F32):
                    return _s5.enter_context(nc.sbuf_tensor(name, shape, dt))
                msk8 = SB5("msk8", [128, 2048])
                cum8 = SB5("cum8", [128, 2048])
                ones8 = SB5("ones8", [128, 2048])
                carry = SB5("carry", [128, 1])
                cinb = SB5("cinb", [128, 128])
                cblk = SB5("cblk", [128, 128])
                p5a = SB5("p5a", [128, 128])
                p5c = SB5("p5c", [128, 128])
                def cc_fn(e, ins_, outs_):
                    return e.collective_compute(
                        "AllGather", ALU.bypass, replica_groups=[list(range(NCORE))],
                        ins=ins_, outs=outs_)
                e_cc1 = EMIT("gpsimd",
                             lambda e: cc_fn(e, [x1_buf[:, :]], [x1_all[:, :]]),
                             waits=(e_x1_done,), inc="CC")
                e_cc2 = EMIT("gpsimd",
                             lambda e: cc_fn(e, [mT_buf[:, :]], [mT_all[:, :]]),
                             waits=(e_mt_done,), inc="CC")
                v_op("memset", ones8[0:8, :], 1.0)
                ev_cr = v_op("memset", carry[0:8, :], 0.0, inc="V")
                e_cumd = None
                for ck in range(NCORE):
                    e_ml = DMA("sync", msk8[0:8, :], mT_all[ck * E:(ck + 1) * E, :],
                               waits=(e_cc2, e_cumd))
                    ev_sc = v_op("tensor_tensor_scan", out=cum8[0:8, :], data0=ones8[0:8, :],
                                 data1=msk8[0:8, :], initial=carry[0:8, 0:1],
                                 op0=ALU.mult, op1=ALU.add, waits=(e_ml,), inc="V")
                    ev_cr = v_op("tensor_copy", out=carry[0:8, :], in_=cum8[0:8, 2047:2048],
                                 inc="V")
                    e_cumd = DMA("sync", cum_d[:, ck * 2048:(ck + 1) * 2048], cum8[0:8, :],
                                 waits=(ev_cr,))
                e_cum_done = e_cumd
                if "cum" in dbg_t:
                    e_cum_done = DMA("gpsimd", dbg_t["cum"][:, :], cum_d[:, :],
                                     waits=(e_cum_done,))
                g_r = core_reg("gpsimd")
                e_cin = DMA("gpsimd", p5c[0:1, :],
                            cum_d[bass.ds(g_r, 1), 127::128], waits=(e_cum_done,),
                            slow=True)
                evb = t_mm(PS[6][:, 0:128], onesr[0:1, :], p5c[0:1, :], True, True,
                           waits=(e_cin,), inc="T")
                ev_cinb = v_op("tensor_copy", out=cinb[:, :], in_=PS[6][:, 0:128],
                               waits=(evb,), inc="V")
                ev_idx = None
                for st_ in range(NCAPT):
                    v_op("tensor_scalar_add", colsm[:, 68:69], i128[:, :], float(st_ * 128),
                         waits=(ev_cinb,))
                    v_op("tensor_scalar", out=p5a[:, :], in0=cinb[:, :],
                         scalar1=colsm[:, 68:69], scalar2=None, op0=ALU.is_le)
                    v_op("tensor_reduce", out=colsm[:, 69:70], in_=p5a[:, :],
                         axis=mybir.AxisListType.X, op=ALU.add)
                    v_op("tensor_scalar_min", colsm[:, 69:70], colsm[:, 69:70], 127.0)
                    ev = v_op("tensor_tensor", out=colsm[:, 70:71], in0=colsm[:, 69:70],
                              in1=meta_sb[:, 0:1], op=ALU.add, inc="V")
                    ev_i = v_op("tensor_copy", out=idx_sb[:, st_:st_ + 1],
                                in_=colsm[:, 70:71], waits=(ev,), inc="V")
                    e_gb = g_opx("indirect_dma_start",
                                 out=cblk[:, :], out_offset=None,
                                 in_=cum_d.rearrange("e (b t) -> (e b) t", t=128),
                                 in_offset=IndirectOffsetOnAxis(ap=idx_sb[:, st_:st_ + 1], axis=0),
                                 waits=(ev_i, e_cum_done), inc="dG", inc_by=16)
                    v_op("tensor_scalar", out=cblk[:, :], in0=cblk[:, :],
                         scalar1=colsm[:, 68:69], scalar2=None, op0=ALU.is_le,
                         waits=(e_gb,))
                    v_op("tensor_reduce", out=colsm[:, 71:72], in_=cblk[:, :],
                         axis=mybir.AxisListType.X, op=ALU.add)
                    v_op("tensor_scalar_mul", colsm[:, 69:70], colsm[:, 69:70], 128.0)
                    v_op("tensor_add", colsm[:, 71:72], colsm[:, 71:72], colsm[:, 69:70])
                    ev = v_op("tensor_scalar_min", colsm[:, 71:72], colsm[:, 71:72],
                              float(NTOT - 1), inc="V")
                    ev_idx = v_op("tensor_copy", out=idx_sb[:, st_:st_ + 1],
                                  in_=colsm[:, 71:72], waits=(ev,), inc="V")
                e_idx_done = ev_idx
                if "idx" in dbg_t:
                    ev_dbg = v_op("tensor_copy", out=p5a[:, 0:20], in_=idx_sb[:, :],
                                  waits=(e_idx_done,), inc="V")
                    EMIT("gpsimd", lambda e: e.dma_start(out=dbg_t["idx"][:, :],
                                                         in_=p5a[:, 0:20]),
                         waits=(ev_dbg,), inc="dG", inc_by=16)

        if phases >= 6:
            phase_sync()
            with ExitStack() as _s6:
                def SB6(name, shape, dt=F32):
                    return _s6.enter_context(nc.sbuf_tensor(name, shape, dt))
                xes = SB6("xes", [128, 1024])
                xefh = SB6("xefh", [128, 8 * 512], BF16)
                xefl = SB6("xefl", [128, 8 * 512], BF16)
                hhi = SB6("hhi", [128, 32 * 512], BF16)
                hlo = SB6("hlo", [128, 32 * 512], BF16)
                w1f = SB6("w1f", [128, 2 * 1024])
                w1h = SB6("w1h", [128, 2 * 1024], BF16)
                w1l = SB6("w1l", [128, 2 * 1024], BF16)
                w2f = SB6("w2f", [128, 2 * 1024])
                w2h = SB6("w2h", [128, 2 * 1024], BF16)
                w2l = SB6("w2l", [128, 2 * 1024], BF16)
                xet_h = SB6("xet_h", [128, 1024], BF16)
                xet_l = SB6("xet_l", [128, 1024], BF16)
                hst = SB6("hst", [128, 512])
                o2st = SB6("o2st", [128, 1024])
                b1c = SB6("b1c", [128, 32])
                b2c = SB6("b2c", [128, 8])
                p6a = SB6("p6a", [128, 1024])
                p6b = SB6("p6b", [128, 2048])
                e_b1 = DMA("sync", b1c[:, :],
                           b1_i.rearrange("z (m p) -> (z p) m", p=128), slow=True)
                e_b2 = DMA("sync", b2c[:, :],
                           b2_i.rearrange("z (m p) -> (z p) m", p=128), slow=True)
                ev_split_free = None
                e_o2 = None
                ev_p6 = {}
                e_xe_free = None
                for cap_c in range(5):
                    ev_xe = None
                    for j in range(4):
                        st_ = cap_c * 4 + j
                        e_g = g_opx("indirect_dma_start",
                                    out=xes[:, :], out_offset=None,
                                    in_=x1_all[:, :],
                                    in_offset=IndirectOffsetOnAxis(ap=idx_sb[:, st_:st_ + 1], axis=0),
                                    waits=(e_idx_done, e_cc1, e_xe_free,
                                           ev_p6.get(("w1mm", 0)), ev_p6.get(("w1mm", 1))),
                                    inc="dG", inc_by=16)
                        ev_t = None
                        for cc in range(8):
                            pst = PS[6][:, 0:128]
                            ev_t = t_tr(pst, xes[:, cc * 128:(cc + 1) * 128],
                                        waits=(e_g, ev_p6.get(6)), inc="T")
                            ev_p6[6] = v_op("tensor_copy",
                                            out=p6a[:, cc * 128:(cc + 1) * 128],
                                            in_=pst,
                                            waits=(ev_t, ev_p6.get("p6a_free")), inc="V")
                        e_xe_free = ev_t
                        if "xed" in dbg_t and cap_c == 0:
                            DMA("gpsimd", dbg_t["xed"][j * 128:(j + 1) * 128, :],
                                xes[:, :], waits=(e_g,))
                        ev1 = v_op("tensor_copy", out=xet_h[:, :], in_=p6a[:, :],
                                   waits=(ev_p6.get(6),), inc="V")
                        v_op("tensor_tensor", out=p6b[:, 0:1024], in0=p6a[:, :],
                             in1=xet_h[:, :], op=ALU.subtract)
                        v_op("tensor_copy", out=xet_l[:, :], in_=p6b[:, 0:1024])
                        v_op("tensor_copy",
                             out=xefh.rearrange("p (cc t) -> p cc t", cc=8)[:, :, j * 128:(j + 1) * 128],
                             in_=xet_h.rearrange("p (cc t) -> p cc t", cc=8))
                        ev_xe = v_op("tensor_copy",
                                     out=xefl.rearrange("p (cc t) -> p cc t", cc=8)[:, :, j * 128:(j + 1) * 128],
                                     in_=xet_l.rearrange("p (cc t) -> p cc t", cc=8),
                                     inc="V")
                    ev_h = None
                    e_w1 = {}
                    for m in range(32):
                        pb = m % 2
                        e_w1[pb] = DMA("sync", w1f[:, pb * 1024:(pb + 1) * 1024].rearrange("p (cc hh) -> p cc hh", cc=8),
                                       w1_i.rearrange("(cc p) hh -> p cc hh", p=128)[:, :, m * 128:(m + 1) * 128],
                                       waits=(ev_p6.get(("w1s", pb)),))
                        eh, el = split(w1f[:, pb * 1024:(pb + 1) * 1024],
                                       w1h[:, pb * 1024:(pb + 1) * 1024],
                                       w1l[:, pb * 1024:(pb + 1) * 1024],
                                       p6b[:, 0:1024],
                                       waits=(e_w1[pb], ev_p6.get(("w1mm", pb))))
                        ev_p6[("w1s", pb)] = el
                        bk = 4 + (m % 2)
                        ev = None
                        for cc in range(8):
                            ev = mm3(PS[bk][:, :],
                                     w1h[:, pb * 1024 + cc * 128: pb * 1024 + (cc + 1) * 128],
                                     w1l[:, pb * 1024 + cc * 128: pb * 1024 + (cc + 1) * 128],
                                     xefh[:, cc * 512:(cc + 1) * 512],
                                     xefl[:, cc * 512:(cc + 1) * 512],
                                     cc == 0, cc == 7,
                                     waits=(ev_xe, el, ev_p6.get(bk)),
                                     inc="T" if cc == 7 else None)
                        ev_p6[("w1mm", pb)] = ev
                        ev_gl = a_op("activation", out=hst[:, :], in_=PS[bk][:, :],
                                     func=AF.Gelu, bias=b1c[:, m:m + 1],
                                     waits=(ev, e_b1, ev_p6.get("hdmp")), inc="A")
                        ev_p6[bk] = ev_gl
                        if "hd" in dbg_t and cap_c == 0:
                            ev_p6["hdmp"] = DMA("gpsimd",
                                                dbg_t["hd"][m * 128:(m + 1) * 128, :],
                                                hst[:, :], waits=(ev_gl,))
                        hsl = slice(m * 512, (m + 1) * 512)
                        _, ev_h = split(hst[:, :], hhi[:, hsl], hlo[:, hsl], p6b[:, 0:512],
                                        waits=(ev_gl, ev_split_free))
                    for dgrp in range(2):
                        ev_o2e = {}
                        for hk in range(32):
                            pb = hk % 2
                            e_w2 = DMA("sync", w2f[:, pb * 1024:(pb + 1) * 1024],
                                       w2_i[hk * 128:(hk + 1) * 128, :],
                                       waits=(ev_p6.get(("w2s", pb)),))
                            eh, el = split(w2f[:, pb * 1024:(pb + 1) * 1024],
                                           w2h[:, pb * 1024:(pb + 1) * 1024],
                                           w2l[:, pb * 1024:(pb + 1) * 1024],
                                           p6b[:, 0:1024],
                                           waits=(e_w2, ev_p6.get(("w2mm", pb))))
                            ev_p6[("w2s", pb)] = el
                            for dd in range(4):
                                d = dgrp * 4 + dd
                                ev = mm3(PS[dd][:, :],
                                         w2h[:, pb * 1024 + d * 128: pb * 1024 + (d + 1) * 128],
                                         w2l[:, pb * 1024 + d * 128: pb * 1024 + (d + 1) * 128],
                                         hhi[:, hk * 512:(hk + 1) * 512],
                                         hlo[:, hk * 512:(hk + 1) * 512],
                                         hk == 0, hk == 31,
                                         waits=(ev_h, el,
                                                ev_p6.get(dd) if hk == 0 else None),
                                         inc="T" if (hk == 31 or dd == 3) else None)
                                if hk == 31:
                                    ev_o2e[dd] = ev
                            ev_p6[("w2mm", pb)] = ev
                        for dd in range(4):
                            d = dgrp * 4 + dd
                            ev_cp = v_op("tensor_scalar", out=o2st[:, 0:512], in0=PS[dd][:, :],
                                         scalar1=b2c[:, d:d + 1], scalar2=None,
                                         op0=ALU.add,
                                         waits=(ev_o2e[dd], e_b2, ev_p6.get("o2st")), inc="V")
                            ev_p6[dd] = ev_cp
                            for j in range(4):
                                pst = PS[7][:, 0:128]
                                evt = t_tr(pst, o2st[:, j * 128:(j + 1) * 128],
                                           waits=(ev_cp, ev_p6.get(7)), inc="T")
                                ev_e = v_op("tensor_copy",
                                            out=p6a[:, j * 128:(j + 1) * 128],
                                            in_=pst,
                                            waits=(evt, ev_p6.get("p6a_free")), inc="V")
                                ev_p6[7] = ev_e
                            ev_p6["o2st"] = ev_e
                            e_o2 = DMA("sync",
                                       out2_b.rearrange("(j p) o -> p j o", p=128)
                                       [:, cap_c * 4: cap_c * 4 + 4, d * 128:(d + 1) * 128],
                                       p6a[:, 0:512].rearrange("p (j t) -> p j t", j=4),
                                       waits=(ev_e,))
                            ev_p6["p6a_free"] = e_o2
                    ev_split_free = ev
                e_ffn_done = e_o2
                if "out2" in dbg_t:
                    e_ffn_done = DMA("gpsimd", dbg_t["out2"][:, :], out2_b[:, :],
                                     waits=(e_ffn_done,))

        class _P7Stop(Exception):
            pass
        if phases >= 7:
            phase_sync()
            with ExitStack() as _s7:
                def SB7(name, shape, dt=F32):
                    return _s7.enter_context(nc.sbuf_tensor(name, shape, dt))
                cumo = SB7("cumo", [128, 2048])
                msko = SB7("msko", [128, 2048])
                posg8 = SB7("posg8", [128, 2048])
                vld8 = SB7("vld8", [128, 2048])
                posgt = SB7("posgt", [128, NTILE * E])
                vldt = SB7("vldt", [128, NTILE * E])
                ridx = SB7("ridx", [128, 2], I32)
                g1 = SB7("g1", [128, 1024])
                g2 = SB7("g2", [128, 1024])
                x1o = SB7("x1o", [128, 1024])
                yt = SB7("yt", [128, 1024])
                rdbg = SB7("rdbg", [128, 2 * NTILE])
                p7a = SB7("p7a", [128, 1024])
                try:
                    if P7SUB < 1:
                        raise _P7Stop()
                    def cc_fn2(e):
                        return e.collective_compute(
                            "AllGather", ALU.bypass, replica_groups=[list(range(NCORE))],
                            ins=[out2_b[:, :]], outs=[out2_all[:, :]])
                    e_cc3 = EMIT("gpsimd", cc_fn2, waits=(e_ffn_done,), inc="CC")

                    if P7SUB < 2:
                        raise _P7Stop()
                    g_g = core_reg("gpsimd")
                    tokb = g_g * TOK
                    e_cl = DMA("gpsimd", cumo[0:8, :], cum_d[:, bass.ds(tokb, TOK)],
                               waits=(e_cum_done,))
                    e_ml = DMA("sync", msko[0:8, :], mT_buf[:, :], waits=(e_mt_done,))
                    v_op("tensor_scalar_add", posg8[0:8, :], cumo[0:8, :], -1.0, waits=(e_cl,))
                    v_op("tensor_scalar", out=vld8[0:8, :], in0=posg8[0:8, :], scalar1=float(CAP),
                         scalar2=None, op0=ALU.is_lt)
                    ev_vd = v_op("tensor_tensor", out=vld8[0:8, :], in0=vld8[0:8, :], in1=msko[0:8, :],
                                 op=ALU.mult, waits=(e_ml,), inc="V")
                    v_op("tensor_scalar_add", posg8[0:8, :], posg8[0:8, :], float(-(CAP - 1)))
                    v_op("tensor_tensor", out=posg8[0:8, :], in0=posg8[0:8, :], in1=vld8[0:8, :],
                         op=ALU.mult)
                    ev_pg = v_op("tensor_scalar_add", posg8[0:8, :], posg8[0:8, :], float(CAP - 1),
                                 inc="V")
                    ev_z = v_op("memset", p7a[:, 0:128], 0.0, inc="V")
                    ev_p7 = {}
                    for tl in range(NTILE):
                        ev1 = v_op("tensor_copy", out=p7a[0:8, 0:128],
                                   in_=posg8[0:8, tl * 128:(tl + 1) * 128],
                                   waits=(ev_pg, ev_z, ev_p7.get("tr")), inc="V")
                        evt = t_tr(PS[6][:, 0:128], p7a[:, 0:128], waits=(ev1, ev_p7.get(6)),
                                   inc="T")
                        ev_p7[6] = v_op("tensor_copy", out=posgt[:, tl * E:(tl + 1) * E],
                                        in_=PS[6][:, 0:E], waits=(evt,), inc="V")
                        ev2 = v_op("tensor_copy", out=p7a[0:8, 0:128],
                                   in_=vld8[0:8, tl * 128:(tl + 1) * 128],
                                   waits=(ev_vd, ev_p7[6]), inc="V")
                        evt = t_tr(PS[6][:, 0:128], p7a[:, 0:128], waits=(ev2,), inc="T")
                        ev_p7["tr"] = v_op("tensor_copy", out=vldt[:, tl * E:(tl + 1) * E],
                                           in_=PS[6][:, 0:E], waits=(evt,), inc="V")
                    ev_tp_done = ev_p7["tr"]

                    e_out = None
                    if P7SUB < 3:
                        raise _P7Stop()
                    for tl in range(NTILE):
                        gsl = slice(tl * E, (tl + 1) * E)
                        v_op("tensor_tensor", out=p7a[:, 0:E], in0=posgt[:, gsl], in1=ec8[:, :],
                             op=ALU.add, waits=(ev_tp_done, e_ec8, ev_p7.get("yt")))
                        v_op("tensor_tensor", out=p7a[:, 8:8 + E], in0=p7a[:, 0:E],
                             in1=oh1_all[:, gsl], op=ALU.mult)
                        v_op("tensor_reduce", out=p7a[:, 16:17], in_=p7a[:, 8:8 + E],
                             axis=mybir.AxisListType.X, op=ALU.add)
                        v_op("tensor_tensor", out=p7a[:, 8:8 + E], in0=p7a[:, 0:E],
                             in1=oh2_all[:, gsl], op=ALU.mult)
                        v_op("tensor_reduce", out=p7a[:, 17:18], in_=p7a[:, 8:8 + E],
                             axis=mybir.AxisListType.X, op=ALU.add)
                        if "ridx" in dbg_t:
                            v_op("tensor_copy", out=rdbg[:, 2 * tl:2 * tl + 2],
                                 in_=p7a[:, 16:18])
                        if DUMMY_GATHER:
                            v_op("memset", p7a[:, 16:18], 0.0)
                        ev_r = v_op("tensor_copy", out=ridx[:, :], in_=p7a[:, 16:18], inc="V")
                        v_op("tensor_tensor", out=p7a[:, 0:E], in0=gates_all[:, gsl],
                             in1=vldt[:, gsl], op=ALU.mult)
                        v_op("tensor_tensor", out=p7a[:, 8:8 + E], in0=p7a[:, 0:E],
                             in1=oh1_all[:, gsl], op=ALU.mult)
                        v_op("tensor_reduce", out=p7a[:, 16:17], in_=p7a[:, 8:8 + E],
                             axis=mybir.AxisListType.X, op=ALU.add)
                        v_op("tensor_tensor", out=p7a[:, 8:8 + E], in0=p7a[:, 0:E],
                             in1=oh2_all[:, gsl], op=ALU.mult)
                        ev_w = v_op("tensor_reduce", out=p7a[:, 17:18], in_=p7a[:, 8:8 + E],
                                    axis=mybir.AxisListType.X, op=ALU.add, inc="V")
                        e_g1 = g_opx("indirect_dma_start",
                                     out=g1[:, :], out_offset=None, in_=out2_all[:, :],
                                     in_offset=IndirectOffsetOnAxis(ap=ridx[:, 0:1], axis=0),
                                     waits=(ev_r, e_cc3), inc="dG", inc_by=16)
                        e_g2 = g_opx("indirect_dma_start",
                                     out=g2[:, :], out_offset=None, in_=out2_all[:, :],
                                     in_offset=IndirectOffsetOnAxis(ap=ridx[:, 1:2], axis=0),
                                     waits=(ev_r,), inc="dG", inc_by=16)
                        e_x1l = DMA("sync", x1o[:, :], x1_buf[tl * 128:(tl + 1) * 128, :],
                                    waits=(e_x1_done, ev_p7.get("x1o")))
                        v_op("tensor_scalar", out=yt[:, :], in0=g1[:, :], scalar1=p7a[:, 16:17],
                             scalar2=None, op0=ALU.mult, waits=(e_g1, ev_w, ev_p7.get("yt")))
                        v_op("tensor_scalar", out=g2[:, :], in0=g2[:, :], scalar1=p7a[:, 17:18],
                             scalar2=None, op0=ALU.mult, waits=(e_g2,))
                        v_op("tensor_add", yt[:, :], yt[:, :], g2[:, :])
                        ev_y = v_op("tensor_tensor", out=yt[:, :], in0=yt[:, :], in1=x1o[:, :],
                                    op=ALU.add, waits=(e_x1l,), inc="V")
                        ev_p7["x1o"] = ev_y
                        ev_sum = v_op("tensor_reduce", out=colsm[:, 64:65], in_=yt[:, :],
                                      axis=mybir.AxisListType.X, op=ALU.add, inc="V")
                        ev_ssq = a_op("activation", out=p7a[:, 0:1024], in_=yt[:, :],
                                      func=AF.Square, accum_out=colsm[:, 65:66],
                                      waits=(ev_y,), inc="A")
                        v_op("tensor_scalar_mul", colsm[:, 64:65], colsm[:, 64:65], 1.0 / C,
                             waits=(ev_sum,))
                        v_op("tensor_scalar_mul", colsm[:, 66:67], colsm[:, 65:66], 1.0 / C,
                             waits=(ev_ssq,))
                        v_op("tensor_mul", colsm[:, 67:68], colsm[:, 64:65], colsm[:, 64:65])
                        ev = v_op("tensor_tensor", out=colsm[:, 66:67], in0=colsm[:, 66:67],
                                  in1=colsm[:, 67:68], op=ALU.subtract, inc="V")
                        ev = a_op("activation", out=colsm[:, 66:67], in_=colsm[:, 66:67],
                                  func=AF.Ln, bias=epsc[:, 0:1], waits=(ev,), inc="A")
                        ev = a_op("activation", out=colsm[:, 66:67], in_=colsm[:, 66:67],
                                  func=AF.Exp, scale=-0.5, inc="A")
                        ev = v_op("tensor_mul", colsm[:, 67:68], colsm[:, 64:65],
                                  colsm[:, 66:67], waits=(ev,))
                        ev_nmr = v_op("tensor_scalar_mul", colsm[:, 67:68], colsm[:, 67:68],
                                      -1.0, inc="V")
                        ev_o = v_op("tensor_scalar", out=yt[:, :], in0=yt[:, :],
                                    scalar1=colsm[:, 66:67], scalar2=colsm[:, 67:68],
                                    op0=ALU.mult, op1=ALU.add, waits=(ev_nmr,), inc="V")
                        ev_p7["yt"] = ev_o
                        e_out = DMA("sync", out_t[tl * 128:(tl + 1) * 128, :], yt[:, :],
                                    waits=(ev_o,))
                    if "ridx" in dbg_t:
                        DMA("gpsimd", dbg_t["ridx"][:, :], rdbg[:, :], waits=(ev_o,))


                except _P7Stop:
                    pass
        final_waits = []
        for sname in ("dS", "dG", "dA", "CC"):
            if sch.semval.get(sname):
                final_waits.append((sname, sch.semval[sname], "gpsimd"))
        EMIT("gpsimd", lambda e: e.nop(), waits=tuple(final_waits))

        sch.replay(block)
    return nc


def kernel(L0_attn_x, L0_attn_wq, L0_attn_wk, L0_attn_wv, L0_attn_wo,
           L0_attn_pope_bias, L0_ffn_router_w, L0_ffn_router_b,
           L0_ffn_w1, L0_ffn_b1, L0_ffn_w2, L0_ffn_b2, dbg=(), phases=99,
           DUMMY_GATHER=False):
    x = np.ascontiguousarray(np.asarray(L0_attn_x, dtype=np.float32))
    nc = build_nc(dbg=dbg, phases=phases, DUMMY_GATHER=DUMMY_GATHER, P7SUB=9)
    in_maps = []
    for c in range(NCORE):
        b = c // 2
        mcol = np.zeros((128, 4), dtype=np.float32)
        mcol[:, 0] = c * 128.0
        in_maps.append({
            "xb": x[b],
            "xq": x[b, (c % 2) * TOK:(c % 2) * TOK + TOK],
            "wq": np.asarray(L0_attn_wq, dtype=np.float32),
            "wk": np.asarray(L0_attn_wk, dtype=np.float32),
            "wv": np.asarray(L0_attn_wv, dtype=np.float32),
            "wo": np.asarray(L0_attn_wo, dtype=np.float32),
            "pope": np.asarray(L0_attn_pope_bias, dtype=np.float32),
            "rw": np.asarray(L0_ffn_router_w, dtype=np.float32),
            "rbv": np.asarray(L0_ffn_router_b, dtype=np.float32).reshape(1, E),
            "w1e": np.asarray(L0_ffn_w1[c], dtype=np.float32),
            "b1e": np.asarray(L0_ffn_b1[c], dtype=np.float32).reshape(1, DHID),
            "w2e": np.asarray(L0_ffn_w2[c], dtype=np.float32),
            "b2e": np.asarray(L0_ffn_b2[c], dtype=np.float32).reshape(1, C),
            "meta": mcol,
        })
    res = run_bass_kernel_spmd(nc, in_maps, core_ids=list(range(NCORE)))
    outs = [np.asarray(res.results[c]["out"]) for c in range(NCORE)]
    full = np.concatenate(outs, axis=0).reshape(B, S, C)
    if dbg:
        return full, res
    return full
